# version 03
# baseline (speedup 1.0000x reference)
from contextlib import ExitStack

import numpy as np
import concourse.bass as bass
import concourse.mybir as mybir

F32 = mybir.dt.float32
BF16 = mybir.dt.bfloat16
U32 = mybir.dt.uint32
AF = mybir.ActivationFunctionType
ALU = mybir.AluOpType
RG = [[0, 1, 2, 3, 4, 5, 6, 7]]

ISL, ISLP, DSL, DSLP = 1376, 1408, 172, 176
NTOK = 1024
NH = 2


class Ctx:
    def __init__(self, nc, stack):
        self.nc = nc
        self.stack = stack
        e = stack.enter_context
        self.xlnb = e(nc.sbuf_tensor("xlnb", [128, 32, NTOK], BF16))
        self.xf = e(nc.sbuf_tensor("xf", [128, 2, NTOK], F32))
        self.wa = e(nc.sbuf_tensor("wa", [128, 2, 1024], BF16))
        self.wb = e(nc.sbuf_tensor("wb", [128, 2, 8, 128], BF16))
        self.zfull = e(nc.sbuf_tensor("zfull", [128, 22, 512], BF16))
        self.zb = e(nc.sbuf_tensor("zb", [128, 8, 512], BF16))
        self.sm = e(nc.sbuf_tensor("sm", [128, 4, NTOK], BF16))
        self.sf = e(nc.sbuf_tensor("sf", [128, 8, 512], F32))
        self.asmb = e(nc.sbuf_tensor("asmb", [128, 1, 32, 128], BF16))
        self.hb = e(nc.sbuf_tensor("hb", [128, 4, 1024], F32))
        self.bias2 = e(nc.sbuf_tensor("bias2", [128, 480], F32))
        self.rows = e(nc.sbuf_tensor("rows", [1, 4096], F32))
        self.bias1 = e(nc.sbuf_tensor("bias1", [128, 128], F32))
        self.ones1 = e(nc.sbuf_tensor("ones1", [1, 128], F32))
        self.onesc = e(nc.sbuf_tensor("onesc", [128, 1], BF16))
        self.eye = e(nc.sbuf_tensor("eye", [128, 128], BF16))
        self.idxt = e(nc.sbuf_tensor("idxt", [128, 2, 512], U32))
        self.nat = e(nc.sbuf_tensor("nat", [128, 2, 4096], BF16))
        self.ps = [e(nc.psum_tensor(f"ps{i}", [128, 512], F32)) for i in range(7)]
        self.pst = e(nc.psum_tensor("pst", [128, 1024], BF16))
        self._sems = {}
        self._bases = {}

    def sem2(self, name, total):
        h = self.sem(name)
        b = self._bases.get(name, 0)
        self._bases[name] = b + total
        return h, b

    def sem(self, name):
        if name not in self._sems:
            self._sems[name] = self.stack.enter_context(self.nc.semaphore(name))
        return self._sems[name]


class DrainV:

    def __init__(self, v):
        self._v = v

    def __getattr__(self, name):
        f = getattr(self._v, name)
        if not callable(f):
            return f

        def g(*a, **k):
            r = f(*a, **k)
            self._v.drain()
            return r
        return g


def emit_const_init(ctx, block, eye_ap):
    cdn = ctx.sem("const_dn")

    @block.sync
    def _(sy):
        sy.dma_start(ctx.eye[:], eye_ap[:, :]).then_inc(cdn, 16)

    @block.vector
    def _(v):
        v.wait_ge(cdn, 16)
        v.memset(ctx.ones1[:], 1.0)
        v.memset(ctx.onesc[:], 1.0)
        v.engine_nop().then_inc(cdn, 1)
    return ("const_dn", 17)


def emit_prep(ctx, block, aps, cast_jobs=(), tp_jobs=()):
    ld = ctx.sem("p_ld")
    cbs = ctx.sem("p_cbs")
    gq = ctx.sem("p_gq")
    tp = ctx.sem("p_tp")
    evd = ctx.sem("p_evd")
    wb2 = ctx.sem("p_wb")
    cc = ctx.sem("p_cc")
    cj = ctx.sem("p_cj")

    chunks = []
    for mat, blob in [(aps["idxg"], aps["wgT_sl"]), (aps["idxu"], aps["wuT_sl"])]:
        for c in range(11):
            chunks.append(dict(idx=mat, r0=c * 128, nj=512, blob=blob, ntp=32, ic=c))
    for c in range(32):
        chunks.append(dict(idx=aps["idxd"], r0=c * 128, nj=DSL, blob=aps["wdT_sl"], ntp=11, ic=c))
    NCHUNK = len(chunks)

    gq_tgt, tp_tgt = [], []
    a1 = a2 = 0
    for ch in chunks:
        a1 += ch["nj"]
        a2 += ch["ntp"]
        gq_tgt.append(16 * a1)
        tp_tgt.append(a2)

    tjobs = []
    for src, dst in tp_jobs:
        R, C = src.shape
        for r0 in range(0, R, 128):
            tjobs.append((src, dst, r0, C // 128))
    tj_tp_tgt = []
    for j, (_, _, _, ntp) in enumerate(tjobs):
        a2 += ntp
        tj_tp_tgt.append(a2)
    NJOB = NCHUNK + len(tjobs)

    ncast = 0
    cast_chunks = []
    for src, dst in cast_jobs:
        R, C = src.shape
        for r0 in range(0, R, 128):
            for c0 in range(0, C, 4096):
                cast_chunks.append((src, dst, r0, c0, min(4096, C - c0)))
                ncast += 1

    @block.sync
    def _(sy):
        for k, ch in enumerate(chunks):
            if k >= 2:
                sy.wait_ge(gq, gq_tgt[k - 2])
            sy.dma_start(ctx.idxt[:, k % 2, 0:ch["nj"]],
                         ch["idx"][ch["r0"]:ch["r0"] + 128, 0:ch["nj"]]).then_inc(ld, 16)
        sy.wait_ge(cj, 32 * ncast)
        for j, (src, dst, r0, ntp) in enumerate(tjobs):
            kk = NCHUNK + j
            if kk >= 2:
                tgt = tp_tgt[kk - 2] if kk - 2 < NCHUNK else tj_tp_tgt[kk - 2 - NCHUNK]
                sy.wait_ge(tp, tgt)
            sy.dma_start(ctx.nat[:, kk % 2, 0:ntp * 128], src[r0:r0 + 128, :]).then_inc(ld, 16)

    @block.gpsimd
    def _(g):
        g.dma_start(ctx.nat[:, 0, :].rearrange("p (a c) -> p a c", c=8),
                    aps["cb"].rearrange("(p a) c -> p a c", p=128)).then_inc(cbs, 16)
        g.wait_ge(cbs, 16)
        g.dma_start(aps["cbb"].rearrange("(p a) c -> p a c", p=128),
                    ctx.nat[:, 0, :].rearrange("p (a c) -> p a c", c=8)).then_inc(cbs, 16)
        g.wait_ge(cbs, 32)
        for k, ch in enumerate(chunks):
            buf = k % 2
            g.wait_ge(ld, 16 * (k + 1))
            if k >= 2:
                g.wait_ge(tp, tp_tgt[k - 2])
            if ch["ntp"] == 11:
                g.memset(ctx.nat[:, buf, ch["nj"] * 8:DSLP * 8], 0.0)
            for j in range(ch["nj"]):
                g.indirect_dma_start(
                    out=ctx.nat[:, buf, j * 8:(j + 1) * 8],
                    out_offset=None,
                    in_=aps["cbb"],
                    in_offset=bass.IndirectOffsetOnAxis(ap=ctx.idxt[:, buf, j:j + 1], axis=0),
                ).then_inc(gq, 16)
        bounce = ctx.xlnb[:, 0:16, :].rearrange("p a b -> p (a b)")
        for cn, (src, dst, r0, c0, cw) in enumerate(cast_chunks):
            b0 = (cn % 2) * 8192
            if cn >= 2:
                g.wait_ge(cj, 32 * (cn - 1))
            g.dma_start(bounce[:, b0:b0 + cw], src[r0:r0 + 128, c0:c0 + cw]).then_inc(cj, 16)
            g.wait_ge(cj, 32 * cn + 16)
            g.dma_start(dst[r0:r0 + 128, c0:c0 + cw], bounce[:, b0:b0 + cw]).then_inc(cj, 16)

    @block.tensor
    def _(t):
        t.wait_ge(ctx.sem("const_dn"), 17)
        n = 0
        for k, ch in enumerate(chunks):
            buf = k % 2
            t.wait_ge(gq, gq_tgt[k])
            if k >= 2:
                t.wait_ge(wb2, 16 * k)
            for dc in range(ch["ntp"]):
                if n >= 1:
                    t.wait_ge(evd, n)
                t.transpose(ctx.pst[:, (n % 4) * 128:(n % 4) * 128 + 128],
                            ctx.nat[:, buf, dc * 128:(dc + 1) * 128],
                            ctx.eye[:, :]).then_inc(tp, 1)
                n += 1
        for j, (src, dst, r0, ntp) in enumerate(tjobs):
            kk = NCHUNK + j
            t.wait_ge(ld, 16 * (kk + 1))
            t.wait_ge(wb2, 16 * kk)
            for dc in range(ntp):
                if n >= 1:
                    t.wait_ge(evd, n)
                t.transpose(ctx.pst[:, (n % 4) * 128:(n % 4) * 128 + 128],
                            ctx.nat[:, kk % 2, dc * 128:(dc + 1) * 128],
                            ctx.eye[:, :]).then_inc(tp, 1)
                n += 1

    @block.scalar
    def _(s):
        n = 0
        for k, ch in enumerate(chunks):
            for dc in range(ch["ntp"]):
                s.wait_ge(tp, n + 1)
                s.activation(out=ctx.asmb[:, 0, dc, :],
                             in_=ctx.pst[:, (n % 4) * 128:(n % 4) * 128 + 128],
                             func=AF.Copy).then_inc(evd, 1)
                n += 1
            s.dma_start(
                ch["blob"].rearrange("(a p) c -> p a c", p=128)[:, 0:ch["ntp"],
                                                               ch["ic"] * 128:(ch["ic"] + 1) * 128],
                ctx.asmb[:, 0, 0:ch["ntp"], :]).then_inc(wb2, 16)
        for j, (src, dst, r0, ntp) in enumerate(tjobs):
            for dc in range(ntp):
                s.wait_ge(tp, n + 1)
                s.activation(out=ctx.asmb[:, 0, dc, :],
                             in_=ctx.pst[:, (n % 4) * 128:(n % 4) * 128 + 128],
                             func=AF.Copy).then_inc(evd, 1)
                n += 1
            s.dma_start(
                dst.rearrange("(a p) c -> p a c", p=128)[:, 0:ntp, r0:r0 + 128],
                ctx.asmb[:, 0, 0:ntp, :]).then_inc(wb2, 16)

    @block.gpsimd
    def _(g):
        g.wait_ge(wb2, 16 * NJOB)
        g.collective_compute("AllGather", ALU.bypass, replica_groups=RG,
                             ins=[aps["wgT_sl"].opt()], outs=[aps["wgT"].opt()]).then_inc(cc)
        g.collective_compute("AllGather", ALU.bypass, replica_groups=RG,
                             ins=[aps["wuT_sl"].opt()], outs=[aps["wuT"].opt()]).then_inc(cc)
        g.collective_compute("AllGather", ALU.bypass, replica_groups=RG,
                             ins=[aps["wdT_sl"].opt()], outs=[aps["wdT"].opt()]).then_inc(cc)
        g.wait_ge(cc, 3)
    return ("p_cc", 3)


def emit_ln(ctx, block, tag, x_ap, waits=(), eps=1e-5):
    xld, XB0 = ctx.sem2("ln_xld", 256)
    xld1, XB1 = ctx.sem2("ln_xld1", 256)
    cst, CB = ctx.sem2("ln_cst", 32)
    sq, QB = ctx.sem2("ln_sq", 32)
    stm, SB = ctx.sem2("ln_stm", 64)
    row, RB = ctx.sem2("ln_row", 5)
    app, AB = ctx.sem2("ln_app", 1)

    @block.scalar
    def _(s):
        for (wname, wv) in waits:
            s.wait_ge(ctx.sem(wname), wv)
        for dc in range(32):
            if dc >= 2:
                s.wait_ge(cst, CB + dc - 1)
            s.dma_start(ctx.xf[:, dc % 2, :],
                        x_ap[dc * 128:(dc + 1) * 128, :]).then_inc(xld if dc % 2 == 0 else xld1, 16)

    @block.vector
    def _(v):
        for dc in range(32):
            v.wait_ge(xld if dc % 2 == 0 else xld1,
                      (XB0 if dc % 2 == 0 else XB1) + 16 * (dc // 2 + 1))
            v.tensor_copy(ctx.xlnb[:, dc, :], ctx.xf[:, dc % 2, :]).then_inc(cst, 1)

    @block.scalar
    def _(s):
        for dc in range(32):
            s.wait_ge(cst, CB + dc + 1)
            if dc >= 2:
                s.wait_ge(stm, SB + 2 * (dc - 1))
            s.activation(out=ctx.sm[:, 2 * (dc % 2), :], in_=ctx.xlnb[:, dc, :],
                         func=AF.Square).then_inc(sq, 1)

    @block.tensor
    def _(t):
        t.wait_ge(ctx.sem("const_dn"), 17)
        for dc in range(32):
            t.wait_ge(sq, QB + dc + 1)
            first, last = dc == 0, dc == 31
            for h in range(NH):
                t.matmul(ctx.ps[0 + h][0:1, :], ctx.onesc[:, 0:1],
                         ctx.xlnb[:, dc, h * 512:(h + 1) * 512], start=first, stop=last)
                t.matmul(ctx.ps[2 + h][0:1, :], ctx.onesc[:, 0:1],
                         ctx.sm[:, 2 * (dc % 2), h * 512:(h + 1) * 512],
                         start=first, stop=last).then_inc(stm, 1)

    R_MU, R_M2, R_VAR, R_RSTD = 0, 1024, 2048, 3072

    @block.vector
    def _(v):
        v.wait_ge(stm, SB + 64)
        for h in range(NH):
            v.tensor_scalar(out=ctx.rows[0:1, R_MU + h * 512:R_MU + h * 512 + 512],
                            in0=ctx.ps[0 + h][0:1, :], scalar1=1.0 / 4096, scalar2=None, op0=ALU.mult)
            v.tensor_scalar(out=ctx.rows[0:1, R_M2 + h * 512:R_M2 + h * 512 + 512],
                            in0=ctx.ps[2 + h][0:1, :], scalar1=1.0 / 4096, scalar2=None, op0=ALU.mult)
        v.tensor_tensor(out=ctx.rows[0:1, R_VAR:R_VAR + 1024], in0=ctx.rows[0:1, R_MU:R_MU + 1024],
                        in1=ctx.rows[0:1, R_MU:R_MU + 1024], op=ALU.mult)
        v.tensor_tensor(out=ctx.rows[0:1, R_VAR:R_VAR + 1024], in0=ctx.rows[0:1, R_M2:R_M2 + 1024],
                        in1=ctx.rows[0:1, R_VAR:R_VAR + 1024], op=ALU.subtract)
        v.tensor_scalar(out=ctx.rows[0:1, R_VAR:R_VAR + 1024], in0=ctx.rows[0:1, R_VAR:R_VAR + 1024],
                        scalar1=float(eps), scalar2=None, op0=ALU.add).then_inc(row, 1)

    @block.scalar
    def _(s):
        s.wait_ge(row, RB + 1)
        s.activation(out=ctx.rows[0:1, R_RSTD:R_RSTD + 1024], in_=ctx.rows[0:1, R_VAR:R_VAR + 1024],
                     func=AF.Sqrt).then_inc(row, 1)

    @block.vector
    def _(v):
        v.wait_ge(row, RB + 2)
        v.reciprocal(ctx.rows[0:1, R_RSTD:R_RSTD + 1024],
                     ctx.rows[0:1, R_RSTD:R_RSTD + 1024]).then_inc(row, 1)

    @block.tensor
    def _(t):
        t.wait_ge(row, RB + 3)
        for h in range(NH):
            t.matmul(ctx.ps[0 + h][:, :], ctx.ones1[0:1, :],
                     ctx.rows[0:1, R_MU + h * 512:R_MU + h * 512 + 512], start=True, stop=True)
            t.matmul(ctx.ps[2 + h][:, :], ctx.ones1[0:1, :],
                     ctx.rows[0:1, R_RSTD + h * 512:R_RSTD + h * 512 + 512],
                     start=True, stop=True).then_inc(row, 1)

    @block.vector
    def _(v):
        v.wait_ge(row, RB + 3 + NH)
        k = 0
        for dc in range(32):
            for h in range(NH):
                slot = 6 + (k % 2)
                v.tensor_tensor(out=ctx.sf[:, slot, :], in0=ctx.xlnb[:, dc, h * 512:(h + 1) * 512],
                                in1=ctx.ps[0 + h][:, :], op=ALU.subtract)
                ins = v.tensor_tensor(out=ctx.xlnb[:, dc, h * 512:(h + 1) * 512], in0=ctx.sf[:, slot, :],
                                      in1=ctx.ps[2 + h][:, :], op=ALU.mult)
                k += 1
        ins.then_inc(app, 1)
    return ("ln_app", AB + 1)


def emit_ffn0(ctx, block, tag, x_ap, xout_ap, wgT, wuT, wdT, z_ap, waits=()):
    ln_done = emit_ln(ctx, block, f"{tag}l", x_ap, waits=waits)
    wld, WB0 = ctx.sem2("f0_wld", 32768)
    wld1, WB1 = ctx.sem2("f0_wld1", 32768)
    mmu, MB = ctx.sem2("f0_mmu", 2048)
    qev, QV = ctx.sem2("f0_qev", 64)
    sil, SL = ctx.sem2("f0_sil", 176)
    zst, ZS = ctx.sem2("f0_zst", 2816)
    zld, ZL = ctx.sem2("f0_zld", 1024)
    dwl, DW0 = ctx.sem2("f0_dwl", 11264)
    dwl1, DW1 = ctx.sem2("f0_dwl1", 11264)
    dcons, DC = ctx.sem2("f0_dcons", 1408)
    dev, DV = ctx.sem2("f0_dev", 64)
    xr, XR0 = ctx.sem2("f0_xr", 512)
    xr1, XR1 = ctx.sem2("f0_xr1", 512)
    xw, XW0 = ctx.sem2("f0_xw", 512)
    xw1, XW1 = ctx.sem2("f0_xw1", 512)
    dnh, DN = ctx.sem2("f0_dn", 1)

    quads = []
    for rank in range(8):
        for q0, qn in [(0, 3), (3, 3), (6, 3), (9, 2)]:
            quads.append((rank, q0, qn))

    @block.sync
    def _(sy):
        sy.wait_ge(ctx.sem(ln_done[0]), ln_done[1])
        n = 0
        for h in range(NH):
            for (rank, q0, qn) in quads:
                for dc in range(32):
                    if n >= 2:
                        sy.wait_ge(mmu, MB + n - 1)
                    r0 = rank * 4096 + dc * 128
                    c0, c1 = q0 * 128, (q0 + qn) * 128
                    wsem = wld if n % 2 == 0 else wld1
                    sy.dma_start(ctx.wa[:, n % 2, 0:qn * 128], wgT[r0:r0 + 128, c0:c1]).then_inc(wsem, 16)
                    sy.dma_start(ctx.wb[:, n % 2, :, :].rearrange("p a b -> p (a b)")[:, 0:qn * 128],
                                 wuT[r0:r0 + 128, c0:c1]).then_inc(wsem, 16)
                    n += 1

    @block.tensor
    def _(t):
        n = 0
        nq = 0
        for h in range(NH):
            for (rank, q0, qn) in quads:
                for dc in range(32):
                    t.wait_ge(wld if n % 2 == 0 else wld1,
                              (WB0 if n % 2 == 0 else WB1) + 32 * (n // 2 + 1))
                    if dc == 0 and nq >= 1:
                        t.wait_ge(qev, QV + nq)
                    first, last = dc == 0, dc == 31
                    for k in range(qn):
                        t.matmul(ctx.ps[k][:, :], ctx.wa[:, n % 2, k * 128:(k + 1) * 128],
                                 ctx.xlnb[:, dc, h * 512:(h + 1) * 512], start=first, stop=last)
                        ins = t.matmul(ctx.ps[3 + k][:, :],
                                       ctx.wb[:, n % 2, :, :].rearrange("p a b -> p (a b)")[:, k * 128:(k + 1) * 128],
                                       ctx.xlnb[:, dc, h * 512:(h + 1) * 512], start=first, stop=last)
                    ins.then_inc(mmu, 1)
                    n += 1
                nq += 1

    allq = [q for h in range(NH) for q in quads]

    @block.scalar
    def _(s):
        smf = ctx.sm[:, :, :].rearrange("p a b -> p (a b)")
        nst = 0
        for nq, (rank, q0, qn) in enumerate(allq):
            s.wait_ge(mmu, MB + 32 * (nq + 1))
            if nq >= 2:
                s.wait_ge(qev, QV + nq - 1)
            s.wait_ge(zst, ZS + 16 * nst)
            base = (nq % 2) * 2048
            for k in range(qn):
                s.activation(out=smf[:, base + k * 512:base + (k + 1) * 512],
                             in_=ctx.ps[k][:, :], func=AF.Silu).then_inc(sil, 1)
            if nq >= 1:
                prank, pq0, pqn = allq[nq - 1]
                ph = (nq - 1) // len(quads)
                s.wait_ge(qev, QV + nq)
                for k in range(pqn):
                    ic = prank * 11 + pq0 + k
                    s.dma_start(z_ap[ic * 128:(ic + 1) * 128, ph * 512:(ph + 1) * 512],
                                ctx.zb[:, nst % 8, :]).then_inc(zst, 16)
                    nst += 1
        lrank, lq0, lqn = allq[-1]
        s.wait_ge(qev, QV + len(allq))
        for k in range(lqn):
            ic = lrank * 11 + lq0 + k
            s.dma_start(z_ap[ic * 128:(ic + 1) * 128, (NH - 1) * 512:NH * 512],
                        ctx.zb[:, nst % 8, :]).then_inc(zst, 16)
            nst += 1

    @block.vector
    def _(v):
        nq = 0
        nsil = 0
        nst = 0
        for h in range(NH):
            for (rank, q0, qn) in quads:
                nsil += qn
                v.wait_ge(sil, SL + nsil)
                base = (nq % 2) * 2048
                smf = ctx.sm[:, :, :].rearrange("p a b -> p (a b)")
                for k in range(qn):
                    if nst >= 8:
                        v.wait_ge(zst, ZS + 16 * (nst - 7))
                    ins = v.tensor_tensor(out=ctx.zb[:, nst % 8, :],
                                          in0=smf[:, base + k * 512:base + (k + 1) * 512],
                                          in1=ctx.ps[3 + k][:, :], op=ALU.mult)
                    nst += 1
                ins.then_inc(qev, 1)
                nq += 1

    NZG = 4

    @block.sync
    def _(sy):
        sy.wait_ge(zst, ZS + 16 * 88 * NH)
        n = 0
        for h in range(NH):
            for hq in range(8):
                for zg in range(NZG):
                    for i22 in range(22):
                        ic = zg * 22 + i22
                        rank, lc = divmod(ic, 11)
                        if n >= 2:
                            sy.wait_ge(dcons, DC + n - 1)
                        r0 = rank * ISLP + lc * 128
                        sy.dma_start(ctx.wa[:, n % 2, 0:512],
                                     wdT[r0:r0 + 128, hq * 512:(hq + 1) * 512]).then_inc(
                                         dwl if n % 2 == 0 else dwl1, 16)
                        n += 1

    @block.scalar
    def _(s):
        n = 0
        nev = 0
        for h in range(NH):
            for hq in range(8):
                for zg in range(NZG):
                    if n >= 1:
                        s.wait_ge(dcons, DC + 22 * n)
                    s.dma_start(ctx.zfull[:, :, :],
                                z_ap.rearrange("(a p) c -> p a c", p=128)[:, zg * 22:(zg + 1) * 22,
                                                                          h * 512:(h + 1) * 512]).then_inc(zld, 16)
                    n += 1
                for k in range(4):
                    dcc = hq * 4 + k
                    if nev >= 2:
                        s.wait_ge(dev, DV + nev - 1)
                    s.dma_start(ctx.xf[:, nev % 2, 0:512],
                                x_ap[dcc * 128:(dcc + 1) * 128, h * 512:(h + 1) * 512]).then_inc(
                                    xr if nev % 2 == 0 else xr1, 16)
                    s.wait_ge(dev, DV + nev + 1)
                    s.dma_start(xout_ap[dcc * 128:(dcc + 1) * 128, h * 512:(h + 1) * 512],
                                ctx.sf[:, 4 + nev % 2, :]).then_inc(xw if nev % 2 == 0 else xw1, 16)
                    nev += 1
        s.wait_ge(xw, XW0 + 16 * NH * 16)
        s.wait_ge(xw1, XW1 + 16 * NH * 16)
        s.nop().then_inc(dnh, 1)

    @block.tensor
    def _(t):
        n = 0
        ng = 0
        for h in range(NH):
            for hq in range(8):
                if h + hq > 0:
                    t.wait_ge(dev, DV + (h * 8 + hq) * 4)
                for zg in range(NZG):
                    ng += 1
                    t.wait_ge(zld, ZL + 16 * ng)
                    for i22 in range(22):
                        ic = zg * 22 + i22
                        t.wait_ge(dwl if n % 2 == 0 else dwl1,
                                  (DW0 if n % 2 == 0 else DW1) + 16 * (n // 2 + 1))
                        first = (zg == 0 and i22 == 0)
                        last = (zg == NZG - 1 and i22 == 21)
                        for k in range(4):
                            ins = t.matmul(ctx.ps[k][:, :], ctx.wa[:, n % 2, k * 128:(k + 1) * 128],
                                           ctx.zfull[:, i22, :], start=first, stop=last)
                        ins.then_inc(dcons, 1)
                        n += 1

    @block.vector
    def _(v):
        nev = 0
        for h in range(NH):
            for hq in range(8):
                v.wait_ge(dcons, DC + (h * 8 + hq + 1) * 88)
                for k in range(4):
                    v.wait_ge(xr if nev % 2 == 0 else xr1,
                              (XR0 if nev % 2 == 0 else XR1) + 16 * (nev // 2 + 1))
                    if nev >= 2:
                        v.wait_ge(xw if nev % 2 == 0 else xw1,
                                  (XW0 if nev % 2 == 0 else XW1) + 16 * ((nev - 2) // 2 + 1))
                    v.scalar_tensor_tensor(out=ctx.sf[:, 4 + nev % 2, :],
                                           in0=ctx.ps[k][:, :], scalar=0.5, in1=ctx.xf[:, nev % 2, 0:512],
                                           op0=ALU.mult, op1=ALU.add).then_inc(dev, 1)
                    nev += 1

    return ("f0_dn", DN + 1)


def emit_attn(ctx, block, tag, x_ap, xout_ap, qkvwb, qkvb_rs, vbias_row, owb, ob_rs,
              qT, kT_sl, kT_full, v_sl, v_full, waits=()):
    ln_done = emit_ln(ctx, block, f"{tag}l", x_ap, waits=waits)
    SCALE = 0.08838834764831845
    bld = ctx.sem(f"{tag}_bld")
    pwl = [ctx.sem(f"{tag}_pwl0"), ctx.sem(f"{tag}_pwl1")]
    pmc = ctx.sem(f"{tag}_pmc")
    pev = ctx.sem(f"{tag}_pev")
    pst = ctx.sem(f"{tag}_pst")
    cc2 = ctx.sem(f"{tag}_cc2")
    akl = [ctx.sem(f"{tag}_akl0"), ctx.sem(f"{tag}_akl1")]
    avl = [ctx.sem(f"{tag}_avl0"), ctx.sem(f"{tag}_avl1")]
    aql = [ctx.sem(f"{tag}_aql0"), ctx.sem(f"{tag}_aql1")]
    amc = ctx.sem(f"{tag}_amc")
    aex = ctx.sem(f"{tag}_aex")
    aep = ctx.sem(f"{tag}_aep")
    opw = [ctx.sem(f"{tag}_opw0"), ctx.sem(f"{tag}_opw1")]
    opc = ctx.sem(f"{tag}_opc")
    odev = ctx.sem(f"{tag}_odev")
    oxr = [ctx.sem(f"{tag}_oxr0"), ctx.sem(f"{tag}_oxr1")]
    oxw = [ctx.sem(f"{tag}_oxw0"), ctx.sem(f"{tag}_oxw1")]

    NT_QK, NT_V = 32, 16
    NT = NT_QK + NT_V

    @block.scalar
    def _(s):
        s.dma_start(ctx.bias1[:, 0:64], qkvb_rs[:, :]).then_inc(bld, 16)
        s.dma_start(ctx.bias1[:, 64:96], vbias_row[:, :]).then_inc(bld, 16)
        s.dma_start(ctx.bias1[:, 96:128], ob_rs[:, :]).then_inc(bld, 16)

    @block.sync
    def _(sy):
        sy.wait_ge(ctx.sem(ln_done[0]), ln_done[1])
        for t in range(NT):
            for dc in range(32):
                n = t * 32 + dc
                if n >= 2:
                    sy.wait_ge(pmc, n - 1)
                if t < NT_QK:
                    h2, ocg = divmod(t, 16)
                    c0 = ocg * 512
                else:
                    hg, tcg = divmod(t - NT_QK, 2)
                    c0 = 8192 + hg * 512
                sy.dma_start(ctx.wa[:, n % 2, 0:512],
                             qkvwb[dc * 128:(dc + 1) * 128, c0:c0 + 512]).then_inc(pwl[n % 2], 16)

    @block.tensor
    def _(t_):
        for t in range(NT):
            if t >= 1:
                t_.wait_ge(pev, t)
            for dc in range(32):
                n = t * 32 + dc
                t_.wait_ge(pwl[n % 2], 16 * (n // 2 + 1))
                first, last = dc == 0, dc == 31
                for k in range(4):
                    if t < NT_QK:
                        h2 = t // 16
                        ins = t_.matmul(ctx.ps[k][:, :], ctx.wa[:, n % 2, k * 128:(k + 1) * 128],
                                        ctx.xlnb[:, dc, h2 * 512:(h2 + 1) * 512],
                                        start=first, stop=last)
                    else:
                        hg, tcg = divmod(t - NT_QK, 2)
                        tc = tcg * 4 + k
                        ins = t_.matmul(ctx.ps[k][:, :], ctx.xlnb[:, dc, tc * 128:(tc + 1) * 128],
                                        ctx.wa[:, n % 2, 0:512], start=first, stop=last)
                ins.then_inc(pmc, 1)

    @block.vector
    def _(v):
        nst = 0
        for t in range(NT):
            v.wait_ge(pmc, 32 * (t + 1))
            v.wait_ge(bld, 48)
            for k in range(4):
                if nst >= 8:
                    v.wait_ge(pst, 16 * (nst - 7))
                if t < NT_QK:
                    h2, ocg = divmod(t, 16)
                    oc = ocg * 4 + k
                    ins = v.tensor_scalar(out=ctx.zb[:, nst % 8, :], in0=ctx.ps[k][:, :],
                                          scalar1=ctx.bias1[:, oc:oc + 1], scalar2=None,
                                          op0=ALU.add)
                else:
                    ins = v.tensor_copy(ctx.zb[:, nst % 8, :], ctx.ps[k][:, :])
                nst += 1
            ins.then_inc(pev, 1)

    @block.scalar
    def _(s):
        nst = 0
        for t in range(NT):
            s.wait_ge(pev, t + 1)
            s.wait_ge(pst, 16 * nst)
            for k in range(4):
                if t < NT_QK:
                    h2, ocg = divmod(t, 16)
                    oc = ocg * 4 + k
                    if oc < 32:
                        dst = qT[oc * 128:(oc + 1) * 128, h2 * 512:(h2 + 1) * 512]
                    else:
                        dst = kT_sl[(oc - 32) * 128:(oc - 31) * 128, h2 * 512:(h2 + 1) * 512]
                else:
                    hg, tcg = divmod(t - NT_QK, 2)
                    tc = tcg * 4 + k
                    dst = v_sl[hg * 1024 + tc * 128:hg * 1024 + (tc + 1) * 128, :]
                s.dma_start(dst, ctx.zb[:, nst % 8, :]).then_inc(pst, 16)
                nst += 1

    @block.gpsimd
    def _(g):
        g.wait_ge(pst, 16 * 4 * NT)
        g.collective_compute("AllGather", ALU.bypass, replica_groups=RG,
                             ins=[kT_sl.opt()], outs=[kT_full.opt()]).then_inc(cc2)
        g.collective_compute("AllGather", ALU.bypass, replica_groups=RG,
                             ins=[v_sl.opt()], outs=[v_full.opt()]).then_inc(cc2)
        g.wait_ge(cc2, 2)

    @block.sync
    def _(sy):
        sy.wait_ge(cc2, 2)
        n = 0
        for h in range(32):
            hg, hin = divmod(h, 4)
            for rank in range(8):
                if n >= 2:
                    sy.wait_ge(amc, (n - 1) * 8)
                sy.dma_start(ctx.wa[:, n % 2, :],
                             kT_full[rank * 4096 + h * 128:rank * 4096 + (h + 1) * 128, :]).then_inc(akl[n % 2], 16)
                sy.dma_start(ctx.wb[:, n % 2, :, :],
                             v_full.rearrange("(x p) c -> p x c", p=128)[:, rank * 64 + hg * 8:rank * 64 + hg * 8 + 8,
                                                                         hin * 128:(hin + 1) * 128]).then_inc(avl[n % 2], 16)
                n += 1

    @block.scalar
    def _(s):
        K = 0
        for h in range(32):
            if h >= 2:
                s.wait_ge(aep, h - 1)
            s.dma_start(ctx.sm[:, h % 2, :], qT[h * 128:(h + 1) * 128, :]).then_inc(aql[h % 2], 16)
            for kvc in range(64):
                for qb in range(2):
                    if K >= 2:
                        s.wait_ge(amc, K - 1)
                    s.wait_ge(ctx.sem(f"{tag}_asc"), 2 * K + qb + 1)
                    s.activation(out=ctx.zb[:, 4 + (K % 2) * 2 + qb, :],
                                 in_=ctx.ps[3 + qb * 2 + (K % 2)][:, :],
                                 func=AF.Exp, scale=SCALE).then_inc(aex, 1)
                K += 1

    @block.tensor
    def _(t_):
        K = 0
        for h in range(32):
            t_.wait_ge(aql[h % 2], 16 * (h // 2 + 1))
            if h >= 1:
                t_.wait_ge(aep, h)
            for rank in range(8):
                piece = h * 8 + rank
                t_.wait_ge(akl[piece % 2], 16 * (piece // 2 + 1))
                t_.wait_ge(avl[piece % 2], 16 * (piece // 2 + 1))
                for lc2 in range(8):
                    if K >= 2:
                        t_.wait_ge(aex, 2 * K - 2)
                    for qb in range(2):
                        t_.matmul(ctx.ps[3 + qb * 2 + (K % 2)][:, :],
                                  ctx.wa[:, piece % 2, lc2 * 128:(lc2 + 1) * 128],
                                  ctx.sm[:, h % 2, qb * 512:(qb + 1) * 512],
                                  start=True, stop=True).then_inc(ctx.sem(f"{tag}_asc"), 1)
                    t_.wait_ge(aex, 2 * K + 2)
                    first = rank == 0 and lc2 == 0
                    last = rank == 7 and lc2 == 7
                    for qb in range(2):
                        t_.matmul(ctx.ps[qb][:, :], ctx.wb[:, piece % 2, lc2, :],
                                  ctx.zb[:, 4 + (K % 2) * 2 + qb, :], start=first, stop=last)
                        ins = t_.matmul(ctx.ps[2][(qb * 32):(qb * 32 + 1), :], ctx.onesc[:, 0:1],
                                        ctx.zb[:, 4 + (K % 2) * 2 + qb, :], start=first, stop=last)
                    ins.then_inc(amc, 1)
                    K += 1
            t_.wait_ge(ctx.sem(f"{tag}_aev2"), h + 1)
            for qb in range(2):
                t_.matmul(ctx.ps[3 + qb * 2][:, :], ctx.ones1[0:1, :],
                          ctx.rows[0:1, (h % 2) * 1024 + qb * 512:(h % 2) * 1024 + (qb + 1) * 512],
                          start=True, stop=True).then_inc(ctx.sem(f"{tag}_abc"), 1)

    aev = ctx.sem(f"{tag}_aev")
    aev2 = ctx.sem(f"{tag}_aev2")
    abc = ctx.sem(f"{tag}_abc")

    @block.vector
    def _(v):
        for h in range(32):
            v.wait_ge(amc, (h + 1) * 64)
            for qb in range(2):
                v.tensor_copy(ctx.rows[0:1, (h % 2) * 1024 + qb * 512:(h % 2) * 1024 + (qb + 1) * 512],
                              ctx.ps[2][qb * 32:qb * 32 + 1, :]).then_inc(aev, 1)
            v.reciprocal(ctx.rows[0:1, (h % 2) * 1024:(h % 2) * 1024 + 1024],
                         ctx.rows[0:1, (h % 2) * 1024:(h % 2) * 1024 + 1024]).then_inc(aev2, 1)
            v.wait_ge(abc, 2 * (h + 1))
            for qb in range(2):
                v.tensor_copy(ctx.sf[:, 4 + qb, :], ctx.ps[qb][:, :])
                v.tensor_tensor(out=ctx.sf[:, 6 + qb, :],
                                in0=ctx.sf[:, 4 + qb, :], in1=ctx.ps[3 + qb * 2][:, :],
                                op=ALU.mult)
                ins = v.tensor_scalar(out=ctx.xlnb[:, h, qb * 512:(qb + 1) * 512],
                                      in0=ctx.sf[:, 6 + qb, :],
                                      scalar1=ctx.bias1[:, 64 + h:65 + h], scalar2=None,
                                      op0=ALU.add)
            ins.then_inc(aep, 1)

    @block.sync
    def _(sy):
        sy.wait_ge(aep, 32)
        n = 0
        for h2 in range(2):
            for oq in range(8):
                for hc in range(32):
                    if n >= 2:
                        sy.wait_ge(opc, n - 1)
                    sy.dma_start(ctx.wa[:, n % 2, 0:512],
                                 owb[hc * 128:(hc + 1) * 128, oq * 512:(oq + 1) * 512]).then_inc(opw[n % 2], 16)
                    n += 1

    @block.tensor
    def _(t_):
        n = 0
        for h2 in range(2):
            for oq in range(8):
                g = h2 * 8 + oq
                if g >= 1:
                    t_.wait_ge(odev, g * 4)
                for hc in range(32):
                    t_.wait_ge(opw[n % 2], 16 * (n // 2 + 1))
                    first, last = hc == 0, hc == 31
                    for k in range(4):
                        ins = t_.matmul(ctx.ps[k][:, :], ctx.wa[:, n % 2, k * 128:(k + 1) * 128],
                                        ctx.xlnb[:, hc, h2 * 512:(h2 + 1) * 512], start=first, stop=last)
                    ins.then_inc(opc, 1)
                    n += 1

    @block.vector
    def _(v):
        nev = 0
        for h2 in range(2):
            for oq in range(8):
                v.wait_ge(opc, (h2 * 8 + oq + 1) * 32)
                for k in range(4):
                    v.wait_ge(oxr[nev % 2], 16 * (nev // 2 + 1))
                    if nev >= 2:
                        v.wait_ge(oxw[nev % 2], 16 * ((nev - 2) // 2 + 1))
                    oc2 = oq * 4 + k
                    v.scalar_tensor_tensor(out=ctx.sf[:, 4 + nev % 2, :], in0=ctx.ps[k][:, :],
                                           scalar=ctx.bias1[:, 96 + oc2:97 + oc2],
                                           in1=ctx.xf[:, nev % 2, 0:512],
                                           op0=ALU.add, op1=ALU.add).then_inc(odev, 1)
                    nev += 1

    @block.scalar
    def _(s):
        nev = 0
        for h2 in range(2):
            for oq in range(8):
                for k in range(4):
                    dcc = oq * 4 + k
                    if nev >= 2:
                        s.wait_ge(odev, nev - 1)
                    s.dma_start(ctx.xf[:, nev % 2, 0:512],
                                x_ap[dcc * 128:(dcc + 1) * 128, h2 * 512:(h2 + 1) * 512]).then_inc(oxr[nev % 2], 16)
                    s.wait_ge(odev, nev + 1)
                    s.dma_start(xout_ap[dcc * 128:(dcc + 1) * 128, h2 * 512:(h2 + 1) * 512],
                                ctx.sf[:, 4 + nev % 2, :]).then_inc(oxw[nev % 2], 16)
                    nev += 1
        s.wait_ge(oxw[0], 16 * 32)
        s.wait_ge(oxw[1], 16 * 32)
        s.nop().then_inc(ctx.sem(f"{tag}_dn"), 1)
    return (f"{tag}_dn", 1)


def emit_ffn1(ctx, block, tag, x_ap, xout_ap, w1qT, scale_rs, bias_rs, waits=()):
    ln_done = emit_ln(ctx, block, f"{tag}l", x_ap, waits=waits)
    bld, BL = ctx.sem2("f1_bld", 32)
    hsd, HS = ctx.sem2("f1_hsd", 1)
    fwl0, FW0 = ctx.sem2("f1_fwl0", 4096)
    fwl1, FW1 = ctx.sem2("f1_fwl1", 4096)
    fmc, FM = ctx.sem2("f1_fmc", 512)
    fdev, FD = ctx.sem2("f1_fdev", 64)
    fxr0, FR0 = ctx.sem2("f1_fxr0", 512)
    fxr1, FR1 = ctx.sem2("f1_fxr1", 512)
    fxw0, FWW0 = ctx.sem2("f1_fxw0", 512)
    fxw1, FWW1 = ctx.sem2("f1_fxw1", 512)
    dnh, DN = ctx.sem2("f1_dn", 1)

    @block.scalar
    def _(s):
        s.dma_start(ctx.bias1[:, 0:32], scale_rs[:, :]).then_inc(bld, 16)
        s.dma_start(ctx.bias1[:, 32:64], bias_rs[:, :]).then_inc(bld, 16)

    @block.vector
    def _(v):
        v.wait_ge(bld, BL + 32)
        v.tensor_scalar(out=ctx.bias1[:, 0:32], in0=ctx.bias1[:, 0:32], scalar1=0.5,
                        scalar2=None, op0=ALU.mult)
        v.tensor_scalar(out=ctx.bias1[:, 32:64], in0=ctx.bias1[:, 32:64], scalar1=0.5,
                        scalar2=None, op0=ALU.mult).then_inc(hsd, 1)

    @block.sync
    def _(sy):
        sy.wait_ge(ctx.sem(ln_done[0]), ln_done[1])
        n = 0
        for h2 in range(2):
            for og in range(8):
                for dc in range(32):
                    if n >= 2:
                        sy.wait_ge(fmc, FM + n - 1)
                    sy.dma_start(ctx.wa[:, n % 2, 0:512],
                                 w1qT[dc * 128:(dc + 1) * 128, og * 512:(og + 1) * 512]).then_inc(
                                     fwl0 if n % 2 == 0 else fwl1, 16)
                    n += 1

    @block.tensor
    def _(t_):
        n = 0
        for h2 in range(2):
            for og in range(8):
                g = h2 * 8 + og
                if g >= 1:
                    t_.wait_ge(fdev, FD + g * 4)
                for dc in range(32):
                    t_.wait_ge(fwl0 if n % 2 == 0 else fwl1,
                               (FW0 if n % 2 == 0 else FW1) + 16 * (n // 2 + 1))
                    first, last = dc == 0, dc == 31
                    for k in range(4):
                        ins = t_.matmul(ctx.ps[k][:, :], ctx.wa[:, n % 2, k * 128:(k + 1) * 128],
                                        ctx.xlnb[:, dc, h2 * 512:(h2 + 1) * 512], start=first, stop=last)
                    ins.then_inc(fmc, 1)
                    n += 1

    @block.vector
    def _(v):
        v.wait_ge(hsd, HS + 1)
        nev = 0
        for h2 in range(2):
            for og in range(8):
                v.wait_ge(fmc, FM + (h2 * 8 + og + 1) * 32)
                for k in range(4):
                    oc = og * 4 + k
                    v.wait_ge(fxr0 if nev % 2 == 0 else fxr1,
                              (FR0 if nev % 2 == 0 else FR1) + 16 * (nev // 2 + 1))
                    if nev >= 2:
                        v.wait_ge(fxw0 if nev % 2 == 0 else fxw1,
                                  (FWW0 if nev % 2 == 0 else FWW1) + 16 * ((nev - 2) // 2 + 1))
                    v.tensor_scalar(out=ctx.sf[:, 6 + nev % 2, :], in0=ctx.ps[k][:, :],
                                    scalar1=ctx.bias1[:, oc:oc + 1],
                                    scalar2=ctx.bias1[:, 32 + oc:33 + oc],
                                    op0=ALU.mult, op1=ALU.add)
                    v.tensor_tensor(out=ctx.sf[:, 4 + nev % 2, :], in0=ctx.sf[:, 6 + nev % 2, :],
                                    in1=ctx.xf[:, nev % 2, 0:512], op=ALU.add).then_inc(fdev, 1)
                    nev += 1

    @block.scalar
    def _(s):
        nev = 0
        for h2 in range(2):
            for og in range(8):
                for k in range(4):
                    dcc = og * 4 + k
                    if nev >= 2:
                        s.wait_ge(fdev, FD + nev - 1)
                    s.dma_start(ctx.xf[:, nev % 2, 0:512],
                                x_ap[dcc * 128:(dcc + 1) * 128, h2 * 512:(h2 + 1) * 512]).then_inc(
                                    fxr0 if nev % 2 == 0 else fxr1, 16)
                    s.wait_ge(fdev, FD + nev + 1)
                    s.dma_start(xout_ap[dcc * 128:(dcc + 1) * 128, h2 * 512:(h2 + 1) * 512],
                                ctx.sf[:, 4 + nev % 2, :]).then_inc(fxw0 if nev % 2 == 0 else fxw1, 16)
                    nev += 1
        s.wait_ge(fxw0, FWW0 + 16 * 32)
        s.wait_ge(fxw1, FWW1 + 16 * 32)
        s.nop().then_inc(dnh, 1)
    return ("f1_dn", DN + 1)


def _ln512(ctx, block, dst_sm, wtile, wc0, btile, bc0, silu, waits_v, eps=1e-5):
    c5h, C5 = ctx.sem2("i5_c", 4)
    q5h, Q5 = ctx.sem2("i5_q", 4)
    t5h, T5 = ctx.sem2("i5_st", 4)
    r5h, R5 = ctx.sem2("i5_r", 5)
    a5h, A5 = ctx.sem2("i5_a", 8)
    s5h, S5 = ctx.sem2("i5_s", 8)

    @block.vector
    def _(v):
        for (sh, sv) in waits_v:
            v.wait_ge(sh, sv)
        for cc in range(4):
            v.tensor_copy(dst_sm[:, cc, :], ctx.hb[:, cc, :]).then_inc(c5h, 1)

    @block.scalar
    def _(s):
        for cc in range(4):
            s.wait_ge(c5h, C5 + cc + 1)
            if cc >= 2:
                s.wait_ge(t5h, T5 + cc - 1)
            s.activation(out=ctx.zb[:, 2 * (cc % 2):2 * (cc % 2) + 2, :].rearrange("p a b -> p (a b)"),
                         in_=dst_sm[:, cc, :], func=AF.Square).then_inc(q5h, 1)

    @block.tensor
    def _(t):
        for cc in range(4):
            t.wait_ge(q5h, Q5 + cc + 1)
            first, last = cc == 0, cc == 3
            sqv = ctx.zb[:, 2 * (cc % 2):2 * (cc % 2) + 2, :].rearrange("p a b -> p (a b)")
            for h2 in range(NH):
                t.matmul(ctx.ps[0 + h2][0:1, :], ctx.onesc[:, 0:1],
                         dst_sm[:, cc, h2 * 512:(h2 + 1) * 512], start=first, stop=last)
                ins = t.matmul(ctx.ps[2 + h2][0:1, :], ctx.onesc[:, 0:1],
                               sqv[:, h2 * 512:(h2 + 1) * 512], start=first, stop=last)
            ins.then_inc(t5h, 1)

    @block.vector
    def _(v):
        v.wait_ge(t5h, T5 + 4)
        for h2 in range(NH):
            v.tensor_scalar(out=ctx.rows[0:1, h2 * 512:h2 * 512 + 512],
                            in0=ctx.ps[0 + h2][0:1, :], scalar1=1.0 / 512, scalar2=None, op0=ALU.mult)
            v.tensor_scalar(out=ctx.rows[0:1, 1024 + h2 * 512:1024 + h2 * 512 + 512],
                            in0=ctx.ps[2 + h2][0:1, :], scalar1=1.0 / 512, scalar2=None, op0=ALU.mult)
        v.tensor_tensor(out=ctx.rows[0:1, 2048:3072], in0=ctx.rows[0:1, 0:1024],
                        in1=ctx.rows[0:1, 0:1024], op=ALU.mult)
        v.tensor_tensor(out=ctx.rows[0:1, 2048:3072], in0=ctx.rows[0:1, 1024:2048],
                        in1=ctx.rows[0:1, 2048:3072], op=ALU.subtract)
        v.tensor_scalar(out=ctx.rows[0:1, 2048:3072], in0=ctx.rows[0:1, 2048:3072],
                        scalar1=float(eps), scalar2=None, op0=ALU.add).then_inc(r5h, 1)

    @block.scalar
    def _(s):
        s.wait_ge(r5h, R5 + 1)
        s.activation(out=ctx.rows[0:1, 3072:4096], in_=ctx.rows[0:1, 2048:3072],
                     func=AF.Sqrt).then_inc(r5h, 1)

    @block.vector
    def _(v):
        v.wait_ge(r5h, R5 + 2)
        v.reciprocal(ctx.rows[0:1, 3072:4096], ctx.rows[0:1, 3072:4096]).then_inc(r5h, 1)

    @block.tensor
    def _(t):
        t.wait_ge(r5h, R5 + 3)
        for h2 in range(NH):
            t.matmul(ctx.ps[0 + h2][:, :], ctx.ones1[0:1, :],
                     ctx.rows[0:1, h2 * 512:h2 * 512 + 512], start=True, stop=True)
            ins = t.matmul(ctx.ps[2 + h2][:, :], ctx.ones1[0:1, :],
                           ctx.rows[0:1, 3072 + h2 * 512:3072 + h2 * 512 + 512],
                           start=True, stop=True)
        ins.then_inc(r5h, 2)

    @block.vector
    def _(v):
        v.wait_ge(r5h, R5 + 5)
        k = 0
        for cc in range(4):
            for h2 in range(NH):
                v.tensor_tensor(out=ctx.sf[:, 0, :], in0=dst_sm[:, cc, h2 * 512:(h2 + 1) * 512],
                                in1=ctx.ps[0 + h2][:, :], op=ALU.subtract)
                v.tensor_tensor(out=ctx.sf[:, 1, :], in0=ctx.sf[:, 0, :],
                                in1=ctx.ps[2 + h2][:, :], op=ALU.mult)
                if silu:
                    if k >= 2:
                        v.wait_ge(s5h, S5 + k - 1)
                    v.tensor_scalar(out=ctx.sf[:, 2 + k % 2, :], in0=ctx.sf[:, 1, :],
                                    scalar1=wtile[:, wc0 + cc:wc0 + cc + 1],
                                    scalar2=btile[:, bc0 + cc:bc0 + cc + 1],
                                    op0=ALU.mult, op1=ALU.add).then_inc(a5h, 1)
                else:
                    v.tensor_scalar(out=dst_sm[:, cc, h2 * 512:(h2 + 1) * 512], in0=ctx.sf[:, 1, :],
                                    scalar1=wtile[:, wc0 + cc:wc0 + cc + 1],
                                    scalar2=btile[:, bc0 + cc:bc0 + cc + 1],
                                    op0=ALU.mult, op1=ALU.add).then_inc(a5h, 1)
                k += 1

    if silu:
        @block.scalar
        def _(s):
            k = 0
            for cc in range(4):
                for h2 in range(NH):
                    s.wait_ge(a5h, A5 + k + 1)
                    s.activation(out=dst_sm[:, cc, h2 * 512:(h2 + 1) * 512],
                                 in_=ctx.sf[:, 2 + k % 2, :], func=AF.Silu).then_inc(s5h, 1)
                    k += 1
        return (s5h, S5 + 8)

    @block.scalar
    def _(s):
        s.wait_ge(a5h, A5 + 8)
        s.nop().then_inc(s5h, 8)
    return (a5h, A5 + 8)


def emit_l1attn(ctx, block, tag, x_ap, xout_ap, winT, w1T, w2T, woutT, prm,
                agA_in, agA_out, waits=()):
    ln_done = emit_ln(ctx, block, f"{tag}l", x_ap, waits=waits)
    B2 = ctx.bias2
    pld = ctx.sem(f"{tag}_pld")
    stp = ctx.sem(f"{tag}_stp")
    drv = ctx.sem(f"{tag}_drv")
    rbd = ctx.sem(f"{tag}_rbd")
    wwl = [ctx.sem(f"{tag}_wwl0"), ctx.sem(f"{tag}_wwl1")]
    wmc = ctx.sem(f"{tag}_wmc")
    pgd = ctx.sem(f"{tag}_pgd")
    gev = ctx.sem(f"{tag}_gev")
    oev = ctx.sem(f"{tag}_oev")
    ast = ctx.sem(f"{tag}_ast")
    ald = ctx.sem(f"{tag}_ald")
    agc = ctx.sem(f"{tag}_agc")
    sxr = [ctx.sem(f"{tag}_sxr0"), ctx.sem(f"{tag}_sxr1")]
    sxw = [ctx.sem(f"{tag}_sxw0"), ctx.sem(f"{tag}_sxw1")]
    dn5 = ctx.sem(f"{tag}_dn")

    X1 = ctx.xf[:, 0, 512:1024]
    X2 = ctx.xf[:, 1, 0:512]
    X3 = ctx.xf[:, 1, 512:1024]

    (C_LN1W, C_LN1B, C_LN2W, C_LN2B, C_SCL, C_PZR, C_PZI, C_LCR, C_LCI, C_B1, C_B2v,
     C_BIN, C_LNLW, C_LNLB, C_BOUT, C_MOH, C_RHO, C_C1, C_S1, C_RHOC, C_RPOW, C_EKC,
     C_EKS, C_E10C, C_E10S, C_CRE, C_CIM, C_SELR, C_SELI, C_T1, C_T2, C_T3, C_T4,
     C_W1) = (0, 16, 32, 48, 64, 80, 96, 112, 128, 144, 208, 224, 228, 232, 236, 268,
              280, 284, 288, 292, 296, 336, 376, 416, 420, 424, 428, 432, 436, 440,
              444, 448, 452, 456)

    def SP_scv(d, cc):
        return 5 * d + cc + 1

    def SP_cor(d):
        return 5 * d + 5

    PL = [("ln1w", C_LN1W, 16), ("ln1b", C_LN1B, 16), ("ln2w", C_LN2W, 16),
          ("ln2b", C_LN2B, 16), ("scl", C_SCL, 16), ("pzr", C_PZR, 16), ("pzi", C_PZI, 16),
          ("lcr", C_LCR, 16), ("lci", C_LCI, 16), ("b1", C_B1, 64), ("b2", C_B2v, 16),
          ("bin", C_BIN, 4), ("lnlw", C_LNLW, 4), ("lnlb", C_LNLB, 4), ("bout", C_BOUT, 32),
          ("maskoh", C_MOH, 8)]
    NPL = 16 * len(PL)

    @block.scalar
    def _(s):
        for name, c0, w in PL:
            s.dma_start(B2[:, c0:c0 + w], prm[name][:, :]).then_inc(pld, 16)

    wload = []
    for h2 in range(NH):
        for dc in range(32):
            wload.append((winT, dc * 128, 0))
    for d in range(4):
        for h2 in range(NH):
            for og in range(4):
                for dc in range(4):
                    wload.append((w1T, d * 512 + dc * 128, og * 512))
            for zc in range(16):
                wload.append((w2T, d * 2048 + zc * 128, 0))
    for h2 in range(NH):
        for og in range(8):
            for dc in range(4):
                wload.append((woutT, dc * 128, og * 512))

    @block.sync
    def _(sy):
        sy.wait_ge(ctx.sem(ln_done[0]), ln_done[1])
        for n, (src, r0, c0) in enumerate(wload):
            if n >= 2:
                sy.wait_ge(wmc, n - 1)
            sy.dma_start(ctx.wa[:, n % 2, 0:512], src[r0:r0 + 128, c0:c0 + 512]).then_inc(wwl[n % 2], 16)

    @block.tensor
    def _(t):
        t.wait_ge(ctx.sem(ln_done[0]), ln_done[1])
        n = 0
        for h2 in range(NH):
            if h2 == 1:
                t.wait_ge(gev, 1)
            for dc in range(32):
                t.wait_ge(wwl[n % 2], 16 * (n // 2 + 1))
                first, last = dc == 0, dc == 31
                for k in range(4):
                    ins = t.matmul(ctx.ps[k][:, :], ctx.wa[:, n % 2, k * 128:(k + 1) * 128],
                                   ctx.xlnb[:, dc, h2 * 512:(h2 + 1) * 512], start=first, stop=last)
                ins.then_inc(wmc, 1)
                n += 1

    @block.vector
    def _(v):
        v.wait_ge(pld, NPL)
        for h2 in range(NH):
            v.wait_ge(wmc, 32 * (h2 + 1))
            for k in range(4):
                ins = v.tensor_scalar(out=ctx.hb[:, k, h2 * 512:(h2 + 1) * 512], in0=ctx.ps[k][:, :],
                                      scalar1=B2[:, C_BIN + k:C_BIN + k + 1], scalar2=None, op0=ALU.add)
            ins.then_inc(gev, 1)

    wtick = [64]
    gcnt = [2]

    def emit_tables(v0, cc, ct, st_, tp1):
        v = DrainV(v0)
        v.memset(ct[:, 0:1], 1.0)
        v.memset(st_[:, 0:1], 0.0)
        for k in range(9):
            sp = 1 << k
            ekc = B2[:, C_EKC + 4 * k + cc:C_EKC + 4 * k + cc + 1]
            eks = B2[:, C_EKS + 4 * k + cc:C_EKS + 4 * k + cc + 1]
            v.tensor_scalar(out=ct[:, sp:2 * sp], in0=ct[:, 0:sp], scalar1=ekc,
                            scalar2=None, op0=ALU.mult)
            v.tensor_scalar(out=tp1[:, 0:sp], in0=st_[:, 0:sp], scalar1=eks,
                            scalar2=None, op0=ALU.mult)
            v.tensor_tensor(out=ct[:, sp:2 * sp], in0=ct[:, sp:2 * sp],
                            in1=tp1[:, 0:sp], op=ALU.subtract)
            v.tensor_scalar(out=st_[:, sp:2 * sp], in0=st_[:, 0:sp], scalar1=ekc,
                            scalar2=None, op0=ALU.mult)
            v.tensor_scalar(out=tp1[:, 0:sp], in0=ct[:, 0:sp], scalar1=eks,
                            scalar2=None, op0=ALU.mult)
            v.tensor_tensor(out=st_[:, sp:2 * sp], in0=st_[:, sp:2 * sp],
                            in1=tp1[:, 0:sp], op=ALU.add)

    def emit_rotate(v0, cc, ct, st_, tp1, tp2a, tp2b):
        v = DrainV(v0)
        e9c = B2[:, C_EKC + 36 + cc:C_EKC + 36 + cc + 1]
        e9s = B2[:, C_EKS + 36 + cc:C_EKS + 36 + cc + 1]
        v.tensor_scalar(out=tp2a[:, :], in0=ct[:, :], scalar1=e9c, scalar2=None, op0=ALU.mult)
        v.tensor_scalar(out=tp2b[:, :], in0=st_[:, :], scalar1=e9s, scalar2=None, op0=ALU.mult)
        v.tensor_scalar(out=tp1[:, :], in0=ct[:, :], scalar1=e9s, scalar2=None, op0=ALU.mult)
        v.tensor_scalar(out=st_[:, :], in0=st_[:, :], scalar1=e9c, scalar2=None, op0=ALU.mult)
        v.tensor_tensor(out=ct[:, :], in0=tp2a[:, :], in1=tp2b[:, :], op=ALU.subtract)
        v.tensor_tensor(out=st_[:, :], in0=st_[:, :], in1=tp1[:, :], op=ALU.add)

    for d in range(4):
        wv = [(gev, 2)] if d == 0 else [(stp, SP_cor(d - 1))]
        dn1 = _ln512(ctx, block, ctx.sm, B2, C_LN1W + d * 4, B2, C_LN1B + d * 4, True, wv)

        @block.vector
        def _(v, d=d):
            v = DrainV(v)
            v.tensor_tensor(out=B2[:, C_T1:C_T1 + 4], in0=B2[:, C_PZR + d * 4:C_PZR + d * 4 + 4],
                            in1=B2[:, C_PZR + d * 4:C_PZR + d * 4 + 4], op=ALU.mult)
            v.tensor_tensor(out=B2[:, C_T2:C_T2 + 4], in0=B2[:, C_PZI + d * 4:C_PZI + d * 4 + 4],
                            in1=B2[:, C_PZI + d * 4:C_PZI + d * 4 + 4], op=ALU.mult)
            v.tensor_tensor(out=B2[:, C_T1:C_T1 + 4], in0=B2[:, C_T1:C_T1 + 4],
                            in1=B2[:, C_T2:C_T2 + 4], op=ALU.add).then_inc(drv, 1)

        @block.scalar
        def _(s, d=d):
            s.wait_ge(drv, 5 * d + 1)
            s.activation(out=B2[:, C_T2:C_T2 + 4], in_=B2[:, C_T1:C_T1 + 4],
                         func=AF.Sqrt).then_inc(drv, 1)

        @block.vector
        def _(v, d=d):
            v.wait_ge(drv, 5 * d + 2)
            v = DrainV(v)
            v.reciprocal(B2[:, C_T3:C_T3 + 4], B2[:, C_T2:C_T2 + 4])
            v.tensor_tensor(out=B2[:, C_C1:C_C1 + 4], in0=B2[:, C_PZR + d * 4:C_PZR + d * 4 + 4],
                            in1=B2[:, C_T3:C_T3 + 4], op=ALU.mult)
            v.tensor_tensor(out=B2[:, C_S1:C_S1 + 4], in0=B2[:, C_PZI + d * 4:C_PZI + d * 4 + 4],
                            in1=B2[:, C_T3:C_T3 + 4], op=ALU.mult).then_inc(drv, 1)

        @block.scalar
        def _(s, d=d):
            s.wait_ge(drv, 5 * d + 3)
            s.activation(out=B2[:, C_RHO:C_RHO + 4], in_=B2[:, C_T2:C_T2 + 4],
                         func=AF.Tanh).then_inc(drv, 1)

        @block.vector
        def _(v, d=d):
            v.wait_ge(drv, 5 * d + 4)
            v = DrainV(v)
            v.tensor_scalar(out=B2[:, C_RPOW:C_RPOW + 4], in0=B2[:, C_RHO:C_RHO + 4],
                            scalar1=1.0, scalar2=None, op0=ALU.mult)
            for k in range(1, 10):
                v.tensor_tensor(out=B2[:, C_RPOW + 4 * k:C_RPOW + 4 * k + 4],
                                in0=B2[:, C_RPOW + 4 * (k - 1):C_RPOW + 4 * (k - 1) + 4],
                                in1=B2[:, C_RPOW + 4 * (k - 1):C_RPOW + 4 * (k - 1) + 4], op=ALU.mult)
            v.tensor_tensor(out=B2[:, C_RHOC:C_RHOC + 4], in0=B2[:, C_RPOW + 36:C_RPOW + 40],
                            in1=B2[:, C_RPOW + 36:C_RPOW + 40], op=ALU.mult)
            v.tensor_scalar(out=B2[:, C_EKC:C_EKC + 4], in0=B2[:, C_C1:C_C1 + 4],
                            scalar1=1.0, scalar2=None, op0=ALU.mult)
            v.tensor_scalar(out=B2[:, C_EKS:C_EKS + 4], in0=B2[:, C_S1:C_S1 + 4],
                            scalar1=1.0, scalar2=None, op0=ALU.mult)
            for k in range(1, 10):
                pc = B2[:, C_EKC + 4 * (k - 1):C_EKC + 4 * (k - 1) + 4]
                ps_ = B2[:, C_EKS + 4 * (k - 1):C_EKS + 4 * (k - 1) + 4]
                v.tensor_tensor(out=B2[:, C_T1:C_T1 + 4], in0=pc, in1=pc, op=ALU.mult)
                v.tensor_tensor(out=B2[:, C_T2:C_T2 + 4], in0=ps_, in1=ps_, op=ALU.mult)
                v.tensor_tensor(out=B2[:, C_EKC + 4 * k:C_EKC + 4 * k + 4],
                                in0=B2[:, C_T1:C_T1 + 4], in1=B2[:, C_T2:C_T2 + 4], op=ALU.subtract)
                v.tensor_tensor(out=B2[:, C_T3:C_T3 + 4], in0=pc, in1=ps_, op=ALU.mult)
                v.tensor_scalar(out=B2[:, C_EKS + 4 * k:C_EKS + 4 * k + 4],
                                in0=B2[:, C_T3:C_T3 + 4], scalar1=2.0, scalar2=None, op0=ALU.mult)
            p9c = B2[:, C_EKC + 36:C_EKC + 40]
            p9s = B2[:, C_EKS + 36:C_EKS + 40]
            v.tensor_tensor(out=B2[:, C_T1:C_T1 + 4], in0=p9c, in1=p9c, op=ALU.mult)
            v.tensor_tensor(out=B2[:, C_T2:C_T2 + 4], in0=p9s, in1=p9s, op=ALU.mult)
            v.tensor_tensor(out=B2[:, C_E10C:C_E10C + 4], in0=B2[:, C_T1:C_T1 + 4],
                            in1=B2[:, C_T2:C_T2 + 4], op=ALU.subtract)
            v.tensor_tensor(out=B2[:, C_T3:C_T3 + 4], in0=p9c, in1=p9s, op=ALU.mult)
            v.tensor_scalar(out=B2[:, C_E10S:C_E10S + 4], in0=B2[:, C_T3:C_T3 + 4],
                            scalar1=2.0, scalar2=None, op0=ALU.mult).then_inc(drv, 1)

        @block.scalar
        def _(s, d=d):
            s.wait_ge(drv, 5 * d + 4)
            for cc in range(4):
                if cc > 0:
                    s.wait_ge(stp, SP_scv(d, cc - 1))
                s.memzero(ctx.sf[:, 4, :])
                s.activation(out=ctx.sf[:, 4, :], in_=ctx.sf[:, 4, :], func=AF.Relu,
                             scale=0.0, bias=B2[:, C_RHO + cc:C_RHO + cc + 1]).then_inc(rbd, 1)

        @block.vector
        def _(v, d=d, dn1=dn1):
            v.wait_ge(dn1[0], dn1[1])
            v.wait_ge(drv, 5 * d + 5)
            v = DrainV(v)
            ct, st_ = ctx.sf[:, 0, :], ctx.sf[:, 1, :]
            tp1 = ctx.sf[:, 2, :]
            rhob = ctx.sf[:, 4, :]
            u_re, u_im = ctx.sf[:, 5, :], ctx.sf[:, 6, :]
            ym = ctx.zb[:, 0, :]
            for cc in range(4):
                v.wait_ge(rbd, 16 * 0 + 4 * d + cc + 1)
                scl_c = B2[:, C_SCL + d * 4 + cc:C_SCL + d * 4 + cc + 1]
                for hh in range(2):
                    if hh == 0:
                        emit_tables(v, cc, ct, st_, tp1)
                    else:
                        emit_rotate(v, cc, ct, st_, tp1, X1, X2)
                    ysl = ctx.sm[:, cc, hh * 512:(hh + 1) * 512]
                    v.tensor_tensor(out=ym[:, :], in0=ysl, in1=ct[:, :], op=ALU.mult)
                    v.drain()
                    v.tensor_tensor_scan(out=u_re[:, :], data0=rhob[:, :], data1=ym[:, :],
                                         initial=(0.0 if hh == 0 else X3[:, 0:1]),
                                         op0=ALU.mult, op1=ALU.add)
                    v.tensor_tensor(out=ym[:, :], in0=ysl, in1=st_[:, :], op=ALU.mult)
                    v.drain()
                    v.tensor_tensor_scan(out=u_im[:, :], data0=rhob[:, :], data1=ym[:, :],
                                         initial=(0.0 if hh == 0 else X3[:, 1:2]),
                                         op0=ALU.mult, op1=ALU.add)
                    v.drain()
                    if hh == 0:
                        v.tensor_copy(X3[:, 0:1], u_re[:, 511:512])
                        v.tensor_copy(X3[:, 1:2], u_im[:, 511:512])
                    else:
                        v.tensor_copy(X3[:, 8 + cc:9 + cc], u_re[:, 511:512])
                        v.tensor_copy(X3[:, 12 + cc:13 + cc], u_im[:, 511:512])
                    v.tensor_tensor(out=X1[:, :], in0=ct[:, :], in1=u_re[:, :], op=ALU.mult)
                    v.tensor_tensor(out=X2[:, :], in0=st_[:, :], in1=u_im[:, :], op=ALU.mult)
                    v.tensor_tensor(out=X1[:, :], in0=X1[:, :], in1=X2[:, :], op=ALU.add)
                    hsl = ctx.hb[:, cc, hh * 512:(hh + 1) * 512]
                    v.scalar_tensor_tensor(out=X2[:, :], in0=X1[:, :], scalar=scl_c,
                                           in1=hsl, op0=ALU.mult, op1=ALU.add)
                    ins = v.tensor_copy(hsl, X2[:, :])
                ins.then_inc(stp, 1)

        @block.scalar
        def _(s, d=d):
            s.wait_ge(stp, SP_scv(d, 3))
            s.dma_start(agA_in[d].rearrange("(a p) x -> p a x", p=128),
                        X3[:, 8:16].rearrange("p (a x) -> p a x", x=1),
                        allow_slow_non_contiguous=True).then_inc(ast, 16)

        @block.gpsimd
        def _(g, d=d):
            g.wait_ge(ast, 16 * (d + 1))
            g.collective_compute("AllGather", ALU.bypass, replica_groups=RG,
                                 ins=[agA_in[d].opt()],
                                 outs=[agA_out[d].opt()]).then_inc(agc, 1)

        @block.scalar
        def _(s, d=d):
            s.wait_ge(agc, d + 1)
            s.dma_start(ctx.sf[:, 0, 0:64].rearrange("p (j a) -> p j a", j=8),
                        agA_out[d].rearrange("(j a p) x -> p j (a x)", j=8, a=8, p=128),
                        allow_slow_non_contiguous=True).then_inc(ald, 16)

        @block.vector
        def _(v, d=d):
            v.wait_ge(ald, 16 * (d + 1))
            v = DrainV(v)
            A = ctx.sf[:, 0, 0:64]
            c1_ = B2[:, C_C1:C_C1 + 4]
            s1_ = B2[:, C_S1:C_S1 + 4]
            lcr = B2[:, C_LCR + d * 4:C_LCR + d * 4 + 4]
            lci = B2[:, C_LCI + d * 4:C_LCI + d * 4 + 4]
            v.tensor_tensor(out=B2[:, C_T1:C_T1 + 4], in0=c1_, in1=lcr, op=ALU.mult)
            v.tensor_tensor(out=B2[:, C_T2:C_T2 + 4], in0=s1_, in1=lci, op=ALU.mult)
            v.tensor_tensor(out=B2[:, C_CRE:C_CRE + 4], in0=B2[:, C_T1:C_T1 + 4],
                            in1=B2[:, C_T2:C_T2 + 4], op=ALU.subtract)
            v.tensor_tensor(out=B2[:, C_T1:C_T1 + 4], in0=s1_, in1=lcr, op=ALU.mult)
            v.tensor_tensor(out=B2[:, C_T2:C_T2 + 4], in0=c1_, in1=lci, op=ALU.mult)
            v.tensor_tensor(out=B2[:, C_T3:C_T3 + 4], in0=B2[:, C_T1:C_T1 + 4],
                            in1=B2[:, C_T2:C_T2 + 4], op=ALU.add)
            v.tensor_scalar(out=B2[:, C_CIM:C_CIM + 4], in0=B2[:, C_T3:C_T3 + 4],
                            scalar1=-1.0, scalar2=None, op0=ALU.mult)
            v.memset(B2[:, C_SELR:C_SELR + 4], 0.0)
            v.memset(B2[:, C_SELI:C_SELI + 4], 0.0)
            rhoc = B2[:, C_RHOC:C_RHOC + 4]
            e10c = B2[:, C_E10C:C_E10C + 4]
            e10s = B2[:, C_E10S:C_E10S + 4]
            for j in range(8):
                oh = B2[:, C_MOH + j:C_MOH + j + 1]
                v.scalar_tensor_tensor(out=B2[:, C_W1:C_W1 + 4], in0=B2[:, C_CRE:C_CRE + 4],
                                       scalar=oh, in1=B2[:, C_SELR:C_SELR + 4],
                                       op0=ALU.mult, op1=ALU.add)
                v.tensor_copy(B2[:, C_SELR:C_SELR + 4], B2[:, C_W1:C_W1 + 4])
                v.scalar_tensor_tensor(out=B2[:, C_W1:C_W1 + 4], in0=B2[:, C_CIM:C_CIM + 4],
                                       scalar=oh, in1=B2[:, C_SELI:C_SELI + 4],
                                       op0=ALU.mult, op1=ALU.add)
                v.tensor_copy(B2[:, C_SELI:C_SELI + 4], B2[:, C_W1:C_W1 + 4])
                if j < 7:
                    v.tensor_tensor(out=B2[:, C_T1:C_T1 + 4], in0=B2[:, C_CRE:C_CRE + 4],
                                    in1=rhoc, op=ALU.mult)
                    v.tensor_tensor(out=B2[:, C_T1:C_T1 + 4], in0=B2[:, C_T1:C_T1 + 4],
                                    in1=A[:, j * 8:j * 8 + 4], op=ALU.add)
                    v.tensor_tensor(out=B2[:, C_T2:C_T2 + 4], in0=B2[:, C_CIM:C_CIM + 4],
                                    in1=rhoc, op=ALU.mult)
                    v.tensor_tensor(out=B2[:, C_T2:C_T2 + 4], in0=B2[:, C_T2:C_T2 + 4],
                                    in1=A[:, j * 8 + 4:j * 8 + 8], op=ALU.add)
                    v.tensor_tensor(out=B2[:, C_T3:C_T3 + 4], in0=B2[:, C_T1:C_T1 + 4],
                                    in1=e10c, op=ALU.mult)
                    v.tensor_tensor(out=B2[:, C_T4:C_T4 + 4], in0=B2[:, C_T2:C_T2 + 4],
                                    in1=e10s, op=ALU.mult)
                    v.tensor_tensor(out=B2[:, C_CRE:C_CRE + 4], in0=B2[:, C_T3:C_T3 + 4],
                                    in1=B2[:, C_T4:C_T4 + 4], op=ALU.add)
                    v.tensor_tensor(out=B2[:, C_T3:C_T3 + 4], in0=B2[:, C_T2:C_T2 + 4],
                                    in1=e10c, op=ALU.mult)
                    v.tensor_tensor(out=B2[:, C_T4:C_T4 + 4], in0=B2[:, C_T1:C_T1 + 4],
                                    in1=e10s, op=ALU.mult)
                    v.tensor_tensor(out=B2[:, C_CIM:C_CIM + 4], in0=B2[:, C_T3:C_T3 + 4],
                                    in1=B2[:, C_T4:C_T4 + 4], op=ALU.subtract)
            ct, st_ = ctx.sf[:, 1, :], ctx.sf[:, 2, :]
            tp1 = ctx.sf[:, 3, :]
            rp = ctx.sf[:, 7, :]
            for cc in range(4):
                cre_c = B2[:, C_SELR + cc:C_SELR + cc + 1]
                cim_c = B2[:, C_SELI + cc:C_SELI + cc + 1]
                scl_c = B2[:, C_SCL + d * 4 + cc:C_SCL + d * 4 + cc + 1]
                for hh in range(2):
                    if hh == 0:
                        emit_tables(v, cc, ct, st_, tp1)
                        v.tensor_scalar(out=rp[:, 0:1], in0=B2[:, C_RHO + cc:C_RHO + cc + 1],
                                        scalar1=1.0, scalar2=None, op0=ALU.mult)
                        for k in range(9):
                            sp = 1 << k
                            v.tensor_scalar(out=rp[:, sp:2 * sp], in0=rp[:, 0:sp],
                                            scalar1=B2[:, C_RPOW + 4 * k + cc:C_RPOW + 4 * k + cc + 1],
                                            scalar2=None, op0=ALU.mult)
                    else:
                        emit_rotate(v, cc, ct, st_, tp1, X1, X2)
                        v.tensor_scalar(out=rp[:, :], in0=rp[:, :],
                                        scalar1=B2[:, C_RPOW + 36 + cc:C_RPOW + 36 + cc + 1],
                                        scalar2=None, op0=ALU.mult)
                    v.tensor_tensor(out=X1[:, :], in0=ct[:, :], in1=rp[:, :], op=ALU.mult)
                    v.tensor_tensor(out=X2[:, :], in0=st_[:, :], in1=rp[:, :], op=ALU.mult)
                    v.tensor_scalar(out=X1[:, :], in0=X1[:, :], scalar1=cre_c, scalar2=None, op0=ALU.mult)
                    v.tensor_scalar(out=X2[:, :], in0=X2[:, :], scalar1=cim_c, scalar2=None, op0=ALU.mult)
                    v.tensor_tensor(out=X1[:, :], in0=X1[:, :], in1=X2[:, :], op=ALU.add)
                    hsl = ctx.hb[:, cc, hh * 512:(hh + 1) * 512]
                    v.scalar_tensor_tensor(out=X2[:, :], in0=X1[:, :], scalar=scl_c,
                                           in1=hsl, op0=ALU.mult, op1=ALU.add)
                    ins = v.tensor_copy(hsl, X2[:, :])
            ins.then_inc(stp, 1)

        dn2 = _ln512(ctx, block, ctx.sm, B2, C_LN2W + d * 4, B2, C_LN2B + d * 4, False,
                     [(stp, SP_cor(d))])

        WT0 = wtick[0]
        G0 = gcnt[0]

        @block.tensor
        def _(t, d=d, dn2=dn2, WT0=WT0, G0=G0):
            t.wait_ge(dn2[0], dn2[1])
            n = WT0
            g = G0
            for h2 in range(NH):
                for og in range(4):
                    t.wait_ge(gev, g)
                    for dc in range(4):
                        t.wait_ge(wwl[n % 2], 16 * (n // 2 + 1))
                        first, last = dc == 0, dc == 3
                        for k in range(4):
                            ins = t.matmul(ctx.ps[k][:, :], ctx.wa[:, n % 2, k * 128:(k + 1) * 128],
                                           ctx.sm[:, dc, h2 * 512:(h2 + 1) * 512], start=first, stop=last)
                        ins.then_inc(wmc, 1)
                        n += 1
                    g += 1
                t.wait_ge(gev, g)
                for zc in range(16):
                    t.wait_ge(wwl[n % 2], 16 * (n // 2 + 1))
                    first, last = zc == 0, zc == 15
                    for k in range(4):
                        ins = t.matmul(ctx.ps[k][:, :], ctx.wa[:, n % 2, k * 128:(k + 1) * 128],
                                       ctx.zfull[:, zc, :], start=first, stop=last)
                    ins.then_inc(wmc, 1)
                    n += 1
                g += 1

        @block.scalar
        def _(s, d=d, G0=G0, WT0=WT0):
            g = G0
            for h2 in range(NH):
                for og in range(4):
                    s.wait_ge(wmc, WT0 + h2 * 32 + og * 4 + 4)
                    for k in range(4):
                        fc = og * 4 + k
                        ins = s.activation(out=ctx.zfull[:, fc, :], in_=ctx.ps[k][:, :], func=AF.Silu,
                                           bias=B2[:, C_B1 + d * 16 + fc:C_B1 + d * 16 + fc + 1],
                                           scale=1.0)
                    ins.then_inc(gev, 1)
                    g += 1
                g += 1

        @block.vector
        def _(v, d=d, G0=G0, WT0=WT0):
            g = G0
            for h2 in range(NH):
                g += 4
                v.wait_ge(wmc, WT0 + h2 * 32 + 32)
                for k in range(4):
                    v.scalar_tensor_tensor(out=ctx.sf[:, 0, :], in0=ctx.ps[k][:, :],
                                           scalar=B2[:, C_B2v + d * 4 + k:C_B2v + d * 4 + k + 1],
                                           in1=ctx.hb[:, k, h2 * 512:(h2 + 1) * 512],
                                           op0=ALU.add, op1=ALU.add)
                    ins = v.tensor_copy(ctx.hb[:, k, h2 * 512:(h2 + 1) * 512], ctx.sf[:, 0, :])
                ins.then_inc(gev, 1)
                g += 1

        wtick[0] = WT0 + 64
        gcnt[0] = G0 + 10

    dnI = _ln512(ctx, block, ctx.sm, B2, C_LNLW, B2, C_LNLB, False, [])
    WT0 = wtick[0]
    G0 = gcnt[0]

    @block.tensor
    def _(t):
        t.wait_ge(dnI[0], dnI[1])
        n = WT0
        g = G0
        for h2 in range(NH):
            for og in range(8):
                lg = h2 * 8 + og
                if lg == 0:
                    t.wait_ge(gev, G0)
                else:
                    t.wait_ge(oev, 4 * lg)
                for dc in range(4):
                    t.wait_ge(wwl[n % 2], 16 * (n // 2 + 1))
                    first, last = dc == 0, dc == 3
                    for k in range(4):
                        ins = t.matmul(ctx.ps[k][:, :], ctx.wa[:, n % 2, k * 128:(k + 1) * 128],
                                       ctx.sm[:, dc, h2 * 512:(h2 + 1) * 512], start=first, stop=last)
                    ins.then_inc(wmc, 1)
                    n += 1

    @block.vector
    def _(v):
        nev = 0
        for h2 in range(NH):
            for og in range(8):
                v.wait_ge(wmc, WT0 + (h2 * 8 + og + 1) * 4)
                for k in range(4):
                    oc = og * 4 + k
                    v.wait_ge(sxr[nev % 2], 16 * (nev // 2 + 1))
                    if nev >= 2:
                        v.wait_ge(sxw[nev % 2], 16 * ((nev - 2) // 2 + 1))
                    v.scalar_tensor_tensor(out=ctx.sf[:, 2 + nev % 2, :], in0=ctx.ps[k][:, :],
                                           scalar=B2[:, C_BOUT + oc:C_BOUT + oc + 1],
                                           in1=ctx.xf[:, nev % 2, 0:512],
                                           op0=ALU.add, op1=ALU.add).then_inc(oev, 1)
                    nev += 1

    @block.scalar
    def _(s):
        nev = 0
        for h2 in range(NH):
            for og in range(8):
                for k in range(4):
                    dcc = og * 4 + k
                    if nev >= 2:
                        s.wait_ge(oev, nev - 1)
                    s.dma_start(ctx.xf[:, nev % 2, 0:512],
                                x_ap[dcc * 128:(dcc + 1) * 128, h2 * 512:(h2 + 1) * 512]).then_inc(
                                    sxr[nev % 2], 16)
                    s.wait_ge(oev, nev + 1)
                    s.dma_start(xout_ap[dcc * 128:(dcc + 1) * 128, h2 * 512:(h2 + 1) * 512],
                                ctx.sf[:, 2 + nev % 2, :]).then_inc(sxw[nev % 2], 16)
                    nev += 1
        s.wait_ge(sxw[0], 16 * 32)
        s.wait_ge(sxw[1], 16 * 32)
        s.nop().then_inc(dn5, 1)
    return (f"{tag}_dn", 1)



def build_full():
    nc = bass.Bass("TRN2", target_bir_lowering=False, debug=False,
                   detect_race_conditions=False)
    F = F32
    x0 = nc.dram_tensor("x0", [4096, NTOK], F, kind="ExternalInput").ap()
    cb = nc.dram_tensor("cb", [65536, 8], F, kind="ExternalInput").ap()
    idxg = nc.dram_tensor("idxg", [ISLP, 512], U32, kind="ExternalInput").ap()
    idxu = nc.dram_tensor("idxu", [ISLP, 512], U32, kind="ExternalInput").ap()
    idxd = nc.dram_tensor("idxd", [4096, DSL], U32, kind="ExternalInput").ap()
    qkvw = nc.dram_tensor("qkvw", [4096, 12288], F, kind="ExternalInput").ap()
    ow = nc.dram_tensor("ow", [4096, 4096], F, kind="ExternalInput").ap()
    qkvb_rs = nc.dram_tensor("qkvb_rs", [128, 64], F, kind="ExternalInput").ap()
    vb_rs = nc.dram_tensor("vb_rs", [128, 32], F, kind="ExternalInput").ap()
    ob_rs = nc.dram_tensor("ob_rs", [128, 32], F, kind="ExternalInput").ap()
    w1q = nc.dram_tensor("w1q", [4096, 4096], mybir.dt.int8, kind="ExternalInput").ap()
    scale_rs = nc.dram_tensor("scale_rs", [128, 32], F, kind="ExternalInput").ap()
    bias_rs = nc.dram_tensor("bias_rs", [128, 32], F, kind="ExternalInput").ap()
    win = nc.dram_tensor("win", [512, 4096], F, kind="ExternalInput").ap()
    w1 = nc.dram_tensor("w1", [8192, 512], F, kind="ExternalInput").ap()
    w2 = nc.dram_tensor("w2", [2048, 2048], F, kind="ExternalInput").ap()
    wout = nc.dram_tensor("wout", [4096, 512], F, kind="ExternalInput").ap()
    prm = {}
    PRM_SHAPES = {"ln1w": 16, "ln1b": 16, "ln2w": 16, "ln2b": 16, "scl": 16, "pzr": 16,
                  "pzi": 16, "lcr": 16, "lci": 16, "b1": 64, "b2": 16, "bin": 4,
                  "lnlw": 4, "lnlb": 4, "bout": 32, "maskoh": 8}
    for n, w in PRM_SHAPES.items():
        prm[n] = nc.dram_tensor(f"p_{n}", [128, w], F, kind="ExternalInput").ap()
    eye = nc.dram_tensor("eye_in", [128, 128], BF16, kind="ExternalInput").ap()
    xout = nc.dram_tensor("xout", [4096, NTOK], F, kind="ExternalOutput").ap()

    cbb = nc.dram_tensor("cbb", [65536, 8], BF16).ap()
    wgT_sl = nc.dram_tensor("wgT_sl", [4096, ISLP], BF16).ap()
    wuT_sl = nc.dram_tensor("wuT_sl", [4096, ISLP], BF16).ap()
    wdT_sl = nc.dram_tensor("wdT_sl", [ISLP, 4096], BF16).ap()
    wgT = nc.dram_tensor("wgT", [8 * 4096, ISLP], BF16).ap()
    wuT = nc.dram_tensor("wuT", [8 * 4096, ISLP], BF16).ap()
    wdT = nc.dram_tensor("wdT", [8 * ISLP, 4096], BF16).ap()
    qkvwb = nc.dram_tensor("qkvwb", [4096, 12288], BF16).ap()
    owb = nc.dram_tensor("owb", [4096, 4096], BF16).ap()
    w1qb = nc.dram_tensor("w1qb", [4096, 4096], BF16).ap()
    w1qT = nc.dram_tensor("w1qT", [4096, 4096], BF16).ap()
    winb = nc.dram_tensor("winb", [512, 4096], BF16).ap()
    winT = nc.dram_tensor("winT", [4096, 512], BF16).ap()
    w1b = nc.dram_tensor("w1b", [8192, 512], BF16).ap()
    w1T = nc.dram_tensor("w1T", [2048, 2048], BF16).ap()
    w2b = nc.dram_tensor("w2b", [2048, 2048], BF16).ap()
    w2T = nc.dram_tensor("w2T", [8192, 512], BF16).ap()
    woutb = nc.dram_tensor("woutb", [4096, 512], BF16).ap()
    woutT = nc.dram_tensor("woutT", [512, 4096], BF16).ap()
    z = nc.dram_tensor("z", [8 * ISLP, NTOK], BF16).ap()
    qT = nc.dram_tensor("qT", [4096, NTOK], BF16).ap()
    kT_sl = nc.dram_tensor("kT_sl", [4096, NTOK], BF16).ap()
    kT_full = nc.dram_tensor("kT_full", [32768, NTOK], BF16).ap()
    v_sl = nc.dram_tensor("v_sl", [8192, 512], BF16).ap()
    v_full = nc.dram_tensor("v_full", [65536, 512], BF16).ap()
    agA_in = [nc.dram_tensor(f"agA_in{d}", [1024, 1], F).ap() for d in range(4)]
    agA_out = [nc.dram_tensor(f"agA_out{d}", [8192, 1], F).ap() for d in range(4)]
    xs = [nc.dram_tensor(f"x{i}", [4096, NTOK], F).ap() for i in range(1, 6)]
    x1, x2, x3, x4, x5 = xs

    prep_aps = dict(cb=cb, idxg=idxg, idxu=idxu, idxd=idxd, cbb=cbb,
                    wgT_sl=wgT_sl, wuT_sl=wuT_sl, wdT_sl=wdT_sl,
                    wgT=wgT, wuT=wuT, wdT=wdT)
    cast_jobs = [(qkvw, qkvwb), (ow, owb), (w1q, w1qb), (win, winb),
                 (w1, w1b), (w2, w2b), (wout, woutb)]
    tp_jobs = [(w1qb, w1qT), (winb, winT), (woutb, woutT)]
    for d in range(4):
        tp_jobs.append((w1b[d * 2048:(d + 1) * 2048, :], w1T[d * 512:(d + 1) * 512, :]))
        tp_jobs.append((w2b[d * 512:(d + 1) * 512, :], w2T[d * 2048:(d + 1) * 2048, :]))

    with ExitStack() as stack:
        ctx = Ctx(nc, stack)
        block = stack.enter_context(nc.Block())
        emit_const_init(ctx, block, eye)
        prep_dn = emit_prep(ctx, block, prep_aps, cast_jobs=cast_jobs, tp_jobs=tp_jobs)
        s1 = emit_ffn0(ctx, block, "s1", x0, x1, wgT, wuT, wdT, z, waits=[prep_dn])
        s2 = emit_attn(ctx, block, "a", x1, x2, qkvwb, qkvb_rs, vb_rs, owb, ob_rs,
                       qT, kT_sl, kT_full, v_sl, v_full, waits=[s1])
        s3 = emit_ffn0(ctx, block, "s3", x2, x3, wgT, wuT, wdT, z, waits=[s2])
        s4 = emit_ffn1(ctx, block, "s4", x3, x4, w1qT, scale_rs, bias_rs, waits=[s3])
        s5 = emit_l1attn(ctx, block, "s5", x4, x5, winT, w1T, w2T, woutT, prm,
                         agA_in, agA_out, waits=[s4])
        s6 = emit_ffn1(ctx, block, "s6", x5, xout, w1qT, scale_rs, bias_rs, waits=[s5])
    return nc


def _rs45(a):
    a = np.asarray(a, np.float32)
    return np.ascontiguousarray(a.reshape(4, 4, 128).transpose(2, 0, 1).reshape(128, 16))


_NC_CACHE = {}


def kernel(**inputs):
    from concourse.bass_utils import run_bass_kernel_spmd

    x = np.asarray(inputs["L0_attn_x"])[0]
    cb = np.ascontiguousarray(np.asarray(inputs["L0_ffn_codebook"], np.float32))
    ig = np.asarray(inputs["L0_ffn_idx_gate"])
    iu = np.asarray(inputs["L0_ffn_idx_up"])
    idn = np.asarray(inputs["L0_ffn_idx_down"])
    qkvw = np.ascontiguousarray(np.asarray(inputs["L0_attn_attn_qkvw"], np.float32))
    qkvb = np.asarray(inputs["L0_attn_attn_qkvb"], np.float32)
    ow = np.ascontiguousarray(np.asarray(inputs["L0_attn_attn_ow"], np.float32))
    ob = np.asarray(inputs["L0_attn_attn_ob"], np.float32)
    w1q = np.ascontiguousarray(np.asarray(inputs["L1_ffn_weight_q"]))
    scale = np.asarray(inputs["L1_ffn_scale"], np.float32)
    bias = np.asarray(inputs["L1_ffn_bias"], np.float32)

    qkvb_rs = np.ascontiguousarray(qkvb[:8192].reshape(64, 128).T)
    vb_rs = np.ascontiguousarray(qkvb[8192:].reshape(32, 128).T)
    ob_rs = np.ascontiguousarray(ob.reshape(32, 128).T)
    scale_rs = np.ascontiguousarray(scale.reshape(-1)[:4096].reshape(32, 128).T)
    bias_rs = np.ascontiguousarray(bias.reshape(32, 128).T)
    win = np.ascontiguousarray(np.asarray(inputs["L1_attn_Win"], np.float32))
    w1 = np.ascontiguousarray(np.asarray(inputs["L1_attn_W1"], np.float32).reshape(8192, 512))
    w2 = np.ascontiguousarray(np.asarray(inputs["L1_attn_W2"], np.float32).reshape(2048, 2048))
    wout = np.ascontiguousarray(np.asarray(inputs["L1_attn_Wout"], np.float32))
    eye = np.eye(128, dtype=np.float32).astype(_bf16_np())

    prm_host = {
        "p_ln1w": _rs45(inputs["L1_attn_ln1_w"]), "p_ln1b": _rs45(inputs["L1_attn_ln1_b"]),
        "p_ln2w": _rs45(inputs["L1_attn_ln2_w"]), "p_ln2b": _rs45(inputs["L1_attn_ln2_b"]),
        "p_scl": _rs45(inputs["L1_attn_sc_lin"]),
        "p_pzr": _rs45(inputs["L1_attn_phz_re"]), "p_pzi": _rs45(inputs["L1_attn_phz_im"]),
        "p_lcr": _rs45(inputs["L1_attn_lc_re"]), "p_lci": _rs45(inputs["L1_attn_lc_im"]),
        "p_b1": np.ascontiguousarray(np.asarray(inputs["L1_attn_b1"], np.float32)
                                     .reshape(4, 16, 128).transpose(2, 0, 1).reshape(128, 64)),
        "p_b2": _rs45(inputs["L1_attn_b2"]),
        "p_bin": np.ascontiguousarray(np.asarray(inputs["L1_attn_bin"], np.float32)
                                      .reshape(4, 128).T),
        "p_lnlw": np.ascontiguousarray(np.asarray(inputs["L1_attn_lnl_w"], np.float32)
                                       .reshape(4, 128).T),
        "p_lnlb": np.ascontiguousarray(np.asarray(inputs["L1_attn_lnl_b"], np.float32)
                                       .reshape(4, 128).T),
        "p_bout": np.ascontiguousarray(np.asarray(inputs["L1_attn_bout"], np.float32)
                                       .reshape(32, 128).T),
    }

    if "nc" not in _NC_CACHE:
        _NC_CACHE["nc"] = build_full()
    nc = _NC_CACHE["nc"]

    in_maps = []
    for r in range(8):
        g = np.pad(ig[r * ISL:(r + 1) * ISL].astype(np.uint32), ((0, ISLP - ISL), (0, 0)))
        u = np.pad(iu[r * ISL:(r + 1) * ISL].astype(np.uint32), ((0, ISLP - ISL), (0, 0)))
        dslice = np.ascontiguousarray(idn[:, r * DSL:(r + 1) * DSL].astype(np.uint32))
        moh = np.zeros((128, 8), np.float32)
        moh[:, r] = 1.0
        m = {"x0": np.ascontiguousarray(x[r * NTOK:(r + 1) * NTOK, :].T),
             "cb": cb, "idxg": g, "idxu": u, "idxd": dslice,
             "qkvw": qkvw, "ow": ow, "qkvb_rs": qkvb_rs, "vb_rs": vb_rs, "ob_rs": ob_rs,
             "w1q": w1q, "scale_rs": scale_rs, "bias_rs": bias_rs,
             "win": win, "w1": w1, "w2": w2, "wout": wout,
             "eye_in": eye.view(np.uint16), "p_maskoh": moh}
        m.update(prm_host)
        in_maps.append(m)
    res = run_bass_kernel_spmd(nc, in_maps, core_ids=list(range(8)))
    out = np.empty((1, 8192, 4096), np.float32)
    for r in range(8):
        out[0, r * NTOK:(r + 1) * NTOK, :] = res.results[r]["xout"].T
    return out


def _bf16_np():
    import ml_dtypes
    return ml_dtypes.bfloat16


# version 05 (final)
# speedup vs baseline: 1.1703x; 1.1703x over previous
from contextlib import ExitStack

import numpy as np
import concourse.bass as bass
import concourse.mybir as mybir

F32 = mybir.dt.float32
BF16 = mybir.dt.bfloat16
U32 = mybir.dt.uint32
AF = mybir.ActivationFunctionType
ALU = mybir.AluOpType
RG = [[0, 1, 2, 3, 4, 5, 6, 7]]

ISL, ISLP, DSL, DSLP = 1376, 1408, 172, 176
NTOK = 1024
NH = 2


class Ctx:
    def __init__(self, nc, stack):
        self.nc = nc
        self.stack = stack
        e = stack.enter_context
        self.xlnb = e(nc.sbuf_tensor("xlnb", [128, 32, NTOK], BF16))
        self.xf = e(nc.sbuf_tensor("xf", [128, 2, NTOK], F32))
        self.wa = e(nc.sbuf_tensor("wa", [128, 2, 1024], BF16))
        self.wb = e(nc.sbuf_tensor("wb", [128, 2, 8, 128], BF16))
        self.zfull = e(nc.sbuf_tensor("zfull", [128, 22, 512], BF16))
        self.zb = e(nc.sbuf_tensor("zb", [128, 8, 512], BF16))
        self.sm = e(nc.sbuf_tensor("sm", [128, 4, NTOK], BF16))
        self.sf = e(nc.sbuf_tensor("sf", [128, 8, 512], F32))
        self.asmb = e(nc.sbuf_tensor("asmb", [128, 1, 32, 128], BF16))
        self.hb = e(nc.sbuf_tensor("hb", [128, 4, 1024], F32))
        self.bias2 = e(nc.sbuf_tensor("bias2", [128, 480], F32))
        self.rows = e(nc.sbuf_tensor("rows", [1, 4096], F32))
        self.bias1 = e(nc.sbuf_tensor("bias1", [128, 128], F32))
        self.ones1 = e(nc.sbuf_tensor("ones1", [1, 128], F32))
        self.onesc = e(nc.sbuf_tensor("onesc", [128, 1], BF16))
        self.eye = e(nc.sbuf_tensor("eye", [128, 128], BF16))
        self.idxt = e(nc.sbuf_tensor("idxt", [128, 2, 512], U32))
        self.nat = e(nc.sbuf_tensor("nat", [128, 2, 4096], BF16))
        self.ps = [e(nc.psum_tensor(f"ps{i}", [128, 512], F32)) for i in range(7)]
        self.pst = e(nc.psum_tensor("pst", [128, 1024], BF16))
        self._sems = {}
        self._bases = {}

    def sem2(self, name, total):
        h = self.sem(name)
        b = self._bases.get(name, 0)
        self._bases[name] = b + total
        return h, b

    def sem(self, name):
        if name not in self._sems:
            self._sems[name] = self.stack.enter_context(self.nc.semaphore(name))
        return self._sems[name]


class DrainV:

    def __init__(self, v):
        self._v = v

    def __getattr__(self, name):
        f = getattr(self._v, name)
        if not callable(f):
            return f

        def g(*a, **k):
            r = f(*a, **k)
            self._v.drain()
            return r
        return g


def emit_const_init(ctx, block, eye_ap):
    cdn = ctx.sem("const_dn")

    @block.sync
    def _(sy):
        sy.dma_start(ctx.eye[:], eye_ap[:, :]).then_inc(cdn, 16)

    @block.vector
    def _(v):
        v.wait_ge(cdn, 16)
        v.memset(ctx.ones1[:], 1.0)
        v.memset(ctx.onesc[:], 1.0)
        v.engine_nop().then_inc(cdn, 1)
    return ("const_dn", 17)


def emit_prep(ctx, block, aps, cast_jobs=(), tp_jobs=()):
    ld = ctx.sem("p_ld")
    cbs = ctx.sem("p_cbs")
    gq = ctx.sem("p_gq")
    tp = ctx.sem("p_tp")
    evd = ctx.sem("p_evd")
    wb2 = ctx.sem("p_wb")
    cc = ctx.sem("p_cc")
    cj = ctx.sem("p_cj")

    chunks = []
    for mat, blob in [(aps["idxg"], aps["wgT_sl"]), (aps["idxu"], aps["wuT_sl"])]:
        for c in range(11):
            chunks.append(dict(idx=mat, r0=c * 128, nj=512, blob=blob, ntp=32, ic=c))
    for c in range(32):
        chunks.append(dict(idx=aps["idxd"], r0=c * 128, nj=DSL, blob=aps["wdT_sl"], ntp=11, ic=c))
    NCHUNK = len(chunks)

    gq_tgt, tp_tgt = [], []
    a1 = a2 = 0
    for ch in chunks:
        a1 += ch["nj"]
        a2 += ch["ntp"]
        gq_tgt.append(16 * a1)
        tp_tgt.append(a2)

    tjobs = []
    for src, dst in tp_jobs:
        R, C = src.shape
        for r0 in range(0, R, 128):
            tjobs.append((src, dst, r0, C // 128))
    tj_tp_tgt = []
    for j, (_, _, _, ntp) in enumerate(tjobs):
        a2 += ntp
        tj_tp_tgt.append(a2)
    NJOB = NCHUNK + len(tjobs)

    ncast = 0
    cast_chunks = []
    for src, dst in cast_jobs:
        R, C = src.shape
        for r0 in range(0, R, 128):
            for c0 in range(0, C, 4096):
                cast_chunks.append((src, dst, r0, c0, min(4096, C - c0)))
                ncast += 1

    @block.sync
    def _(sy):
        for k, ch in enumerate(chunks):
            if k >= 2:
                sy.wait_ge(gq, gq_tgt[k - 2])
            sy.dma_start(ctx.idxt[:, k % 2, 0:ch["nj"]],
                         ch["idx"][ch["r0"]:ch["r0"] + 128, 0:ch["nj"]]).then_inc(ld, 16)
        sy.wait_ge(cj, 32 * ncast)
        for j, (src, dst, r0, ntp) in enumerate(tjobs):
            kk = NCHUNK + j
            if kk >= 2:
                tgt = tp_tgt[kk - 2] if kk - 2 < NCHUNK else tj_tp_tgt[kk - 2 - NCHUNK]
                sy.wait_ge(tp, tgt)
            sy.dma_start(ctx.nat[:, kk % 2, 0:ntp * 128], src[r0:r0 + 128, :]).then_inc(ld, 16)

    ctx.prep_totals = {"cj": 32 * ncast, "wb": 16 * NJOB}

    @block.gpsimd
    def _(g):
        bounce = ctx.xlnb[:, 0:16, :].rearrange("p a b -> p (a b)")
        cast_cn = [0]

        def emit_casts(upto):
            while cast_cn[0] < min(upto, ncast):
                cn = cast_cn[0]
                src, dst, r0, c0, cw = cast_chunks[cn]
                b0 = (cn % 2) * 8192
                if cn >= 2:
                    g.wait_ge(cj, 32 * (cn - 1))
                g.dma_start(bounce[:, b0:b0 + cw], src[r0:r0 + 128, c0:c0 + cw]).then_inc(cj, 16)
                g.wait_ge(cj, 32 * cn + 16)
                g.dma_start(dst[r0:r0 + 128, c0:c0 + cw], bounce[:, b0:b0 + cw]).then_inc(cj, 16)
                cast_cn[0] += 1

        g.dma_start(ctx.nat[:, 0, :].rearrange("p (a c) -> p a c", c=8),
                    aps["cb"].rearrange("(p a) c -> p a c", p=128)).then_inc(cbs, 16)
        g.wait_ge(cbs, 16)
        g.dma_start(aps["cbb"].rearrange("(p a) c -> p a c", p=128),
                    ctx.nat[:, 0, :].rearrange("p (a c) -> p a c", c=8)).then_inc(cbs, 16)
        g.wait_ge(cbs, 32)
        for k, ch in enumerate(chunks):
            buf = k % 2
            g.wait_ge(ld, 16 * (k + 1))
            if k >= 2:
                g.wait_ge(tp, tp_tgt[k - 2])
            if ch["ntp"] == 11:
                g.memset(ctx.nat[:, buf, ch["nj"] * 8:DSLP * 8], 0.0)
            for j in range(ch["nj"]):
                g.indirect_dma_start(
                    out=ctx.nat[:, buf, j * 8:(j + 1) * 8],
                    out_offset=None,
                    in_=aps["cbb"],
                    in_offset=bass.IndirectOffsetOnAxis(ap=ctx.idxt[:, buf, j:j + 1], axis=0),
                ).then_inc(gq, 16)
        emit_casts(ncast)

    @block.tensor
    def _(t):
        t.wait_ge(ctx.sem("const_dn"), 17)
        n = 0
        for k, ch in enumerate(chunks):
            buf = k % 2
            t.wait_ge(gq, gq_tgt[k])
            if k >= 2:
                t.wait_ge(wb2, 16 * k)
            for dc in range(ch["ntp"]):
                if n >= 1:
                    t.wait_ge(evd, n)
                t.transpose(ctx.pst[:, (n % 4) * 128:(n % 4) * 128 + 128],
                            ctx.nat[:, buf, dc * 128:(dc + 1) * 128],
                            ctx.eye[:, :]).then_inc(tp, 1)
                n += 1
        for j, (src, dst, r0, ntp) in enumerate(tjobs):
            kk = NCHUNK + j
            t.wait_ge(ld, 16 * (kk + 1))
            t.wait_ge(wb2, 16 * kk)
            for dc in range(ntp):
                if n >= 1:
                    t.wait_ge(evd, n)
                t.transpose(ctx.pst[:, (n % 4) * 128:(n % 4) * 128 + 128],
                            ctx.nat[:, kk % 2, dc * 128:(dc + 1) * 128],
                            ctx.eye[:, :]).then_inc(tp, 1)
                n += 1

    @block.scalar
    def _(s):
        n = 0
        for k, ch in enumerate(chunks):
            for dc in range(ch["ntp"]):
                s.wait_ge(tp, n + 1)
                s.activation(out=ctx.asmb[:, 0, dc, :],
                             in_=ctx.pst[:, (n % 4) * 128:(n % 4) * 128 + 128],
                             func=AF.Copy).then_inc(evd, 1)
                n += 1
            s.dma_start(
                ch["blob"].rearrange("(a p) c -> p a c", p=128)[:, 0:ch["ntp"],
                                                               ch["ic"] * 128:(ch["ic"] + 1) * 128],
                ctx.asmb[:, 0, 0:ch["ntp"], :]).then_inc(wb2, 16)
        for j, (src, dst, r0, ntp) in enumerate(tjobs):
            for dc in range(ntp):
                s.wait_ge(tp, n + 1)
                s.activation(out=ctx.asmb[:, 0, dc, :],
                             in_=ctx.pst[:, (n % 4) * 128:(n % 4) * 128 + 128],
                             func=AF.Copy).then_inc(evd, 1)
                n += 1
            s.dma_start(
                dst.rearrange("(a p) c -> p a c", p=128)[:, 0:ntp, r0:r0 + 128],
                ctx.asmb[:, 0, 0:ntp, :]).then_inc(wb2, 16)

    @block.gpsimd
    def _(g):
        g.wait_ge(wb2, 16 * NJOB)
        g.collective_compute("AllGather", ALU.bypass, replica_groups=RG,
                             ins=[aps["wgT_sl"].opt()], outs=[aps["wgT"].opt()]).then_inc(cc)
        g.collective_compute("AllGather", ALU.bypass, replica_groups=RG,
                             ins=[aps["wuT_sl"].opt()], outs=[aps["wuT"].opt()]).then_inc(cc)
        g.collective_compute("AllGather", ALU.bypass, replica_groups=RG,
                             ins=[aps["wdT_sl"].opt()], outs=[aps["wdT"].opt()]).then_inc(cc)
        g.wait_ge(cc, 3)
    return ("p_cc", 3)


def emit_ln(ctx, block, tag, x_ap, waits=(), eps=1e-5):
    xld, XB0 = ctx.sem2("ln_xld", 256)
    xld1, XB1 = ctx.sem2("ln_xld1", 256)
    cst, CB = ctx.sem2("ln_cst", 32)
    sq, QB = ctx.sem2("ln_sq", 32)
    stm, SB = ctx.sem2("ln_stm", 64)
    row, RB = ctx.sem2("ln_row", 5)
    app, AB = ctx.sem2("ln_app", 1)

    @block.scalar
    def _(s):
        for (wname, wv) in waits:
            s.wait_ge(ctx.sem(wname), wv)
        for dc in range(32):
            if dc >= 2:
                s.wait_ge(cst, CB + dc - 1)
            s.dma_start(ctx.xf[:, dc % 2, :],
                        x_ap[dc * 128:(dc + 1) * 128, :]).then_inc(xld if dc % 2 == 0 else xld1, 16)

    @block.vector
    def _(v):
        for dc in range(32):
            v.wait_ge(xld if dc % 2 == 0 else xld1,
                      (XB0 if dc % 2 == 0 else XB1) + 16 * (dc // 2 + 1))
            v.tensor_copy(ctx.xlnb[:, dc, :], ctx.xf[:, dc % 2, :]).then_inc(cst, 1)

    @block.scalar
    def _(s):
        for dc in range(32):
            s.wait_ge(cst, CB + dc + 1)
            if dc >= 2:
                s.wait_ge(stm, SB + 2 * (dc - 1))
            s.activation(out=ctx.sm[:, 2 * (dc % 2), :], in_=ctx.xlnb[:, dc, :],
                         func=AF.Square).then_inc(sq, 1)

    @block.tensor
    def _(t):
        t.wait_ge(ctx.sem("const_dn"), 17)
        for dc in range(32):
            t.wait_ge(sq, QB + dc + 1)
            first, last = dc == 0, dc == 31
            for h in range(NH):
                t.matmul(ctx.ps[0 + h][0:1, :], ctx.onesc[:, 0:1],
                         ctx.xlnb[:, dc, h * 512:(h + 1) * 512], start=first, stop=last)
                t.matmul(ctx.ps[2 + h][0:1, :], ctx.onesc[:, 0:1],
                         ctx.sm[:, 2 * (dc % 2), h * 512:(h + 1) * 512],
                         start=first, stop=last).then_inc(stm, 1)

    R_MU, R_M2, R_VAR, R_RSTD = 0, 1024, 2048, 3072

    @block.vector
    def _(v):
        v.wait_ge(stm, SB + 64)
        for h in range(NH):
            v.tensor_scalar(out=ctx.rows[0:1, R_MU + h * 512:R_MU + h * 512 + 512],
                            in0=ctx.ps[0 + h][0:1, :], scalar1=1.0 / 4096, scalar2=None, op0=ALU.mult)
            v.tensor_scalar(out=ctx.rows[0:1, R_M2 + h * 512:R_M2 + h * 512 + 512],
                            in0=ctx.ps[2 + h][0:1, :], scalar1=1.0 / 4096, scalar2=None, op0=ALU.mult)
        v.tensor_tensor(out=ctx.rows[0:1, R_VAR:R_VAR + 1024], in0=ctx.rows[0:1, R_MU:R_MU + 1024],
                        in1=ctx.rows[0:1, R_MU:R_MU + 1024], op=ALU.mult)
        v.tensor_tensor(out=ctx.rows[0:1, R_VAR:R_VAR + 1024], in0=ctx.rows[0:1, R_M2:R_M2 + 1024],
                        in1=ctx.rows[0:1, R_VAR:R_VAR + 1024], op=ALU.subtract)
        v.tensor_scalar(out=ctx.rows[0:1, R_VAR:R_VAR + 1024], in0=ctx.rows[0:1, R_VAR:R_VAR + 1024],
                        scalar1=float(eps), scalar2=None, op0=ALU.add).then_inc(row, 1)

    @block.scalar
    def _(s):
        s.wait_ge(row, RB + 1)
        s.activation(out=ctx.rows[0:1, R_RSTD:R_RSTD + 1024], in_=ctx.rows[0:1, R_VAR:R_VAR + 1024],
                     func=AF.Sqrt).then_inc(row, 1)

    @block.vector
    def _(v):
        v.wait_ge(row, RB + 2)
        v.reciprocal(ctx.rows[0:1, R_RSTD:R_RSTD + 1024],
                     ctx.rows[0:1, R_RSTD:R_RSTD + 1024]).then_inc(row, 1)

    @block.tensor
    def _(t):
        t.wait_ge(row, RB + 3)
        for h in range(NH):
            t.matmul(ctx.ps[0 + h][:, :], ctx.ones1[0:1, :],
                     ctx.rows[0:1, R_MU + h * 512:R_MU + h * 512 + 512], start=True, stop=True)
            t.matmul(ctx.ps[2 + h][:, :], ctx.ones1[0:1, :],
                     ctx.rows[0:1, R_RSTD + h * 512:R_RSTD + h * 512 + 512],
                     start=True, stop=True).then_inc(row, 1)

    @block.vector
    def _(v):
        v.wait_ge(row, RB + 3 + NH)
        k = 0
        for dc in range(32):
            for h in range(NH):
                slot = 6 + (k % 2)
                v.tensor_tensor(out=ctx.sf[:, slot, :], in0=ctx.xlnb[:, dc, h * 512:(h + 1) * 512],
                                in1=ctx.ps[0 + h][:, :], op=ALU.subtract)
                ins = v.tensor_tensor(out=ctx.xlnb[:, dc, h * 512:(h + 1) * 512], in0=ctx.sf[:, slot, :],
                                      in1=ctx.ps[2 + h][:, :], op=ALU.mult)
                k += 1
        ins.then_inc(app, 1)
    return ("ln_app", AB + 1)


def emit_ffn0(ctx, block, tag, x_ap, xout_ap, wgT, wuT, wdT, z_ap, waits=()):
    ln_done = emit_ln(ctx, block, f"{tag}l", x_ap, waits=waits)
    wld, WB0 = ctx.sem2("f0_wld", 32768)
    wld1, WB1 = ctx.sem2("f0_wld1", 32768)
    mmu, MB = ctx.sem2("f0_mmu", 2048)
    qev, QV = ctx.sem2("f0_qev", 64)
    sil, SL = ctx.sem2("f0_sil", 176)
    zst, ZS = ctx.sem2("f0_zst", 2816)
    zld, ZL = ctx.sem2("f0_zld", 1024)
    dwl, DW0 = ctx.sem2("f0_dwl", 11264)
    dwl1, DW1 = ctx.sem2("f0_dwl1", 11264)
    dcons, DC = ctx.sem2("f0_dcons", 1408)
    dev, DV = ctx.sem2("f0_dev", 64)
    xr, XR0 = ctx.sem2("f0_xr", 512)
    xr1, XR1 = ctx.sem2("f0_xr1", 512)
    xw, XW0 = ctx.sem2("f0_xw", 512)
    xw1, XW1 = ctx.sem2("f0_xw1", 512)
    dnh, DN = ctx.sem2("f0_dn", 1)

    quads = []
    for rank in range(8):
        for q0, qn in [(0, 3), (3, 3), (6, 3), (9, 2)]:
            quads.append((rank, q0, qn))

    @block.sync
    def _(sy):
        sy.wait_ge(ctx.sem(ln_done[0]), ln_done[1])
        n = 0
        for h in range(NH):
            for (rank, q0, qn) in quads:
                for dc in range(32):
                    if n >= 2:
                        sy.wait_ge(mmu, MB + n - 1)
                    r0 = rank * 4096 + dc * 128
                    c0, c1 = q0 * 128, (q0 + qn) * 128
                    wsem = wld if n % 2 == 0 else wld1
                    sy.dma_start(ctx.wa[:, n % 2, 0:qn * 128], wgT[r0:r0 + 128, c0:c1]).then_inc(wsem, 16)
                    sy.dma_start(ctx.wb[:, n % 2, :, :].rearrange("p a b -> p (a b)")[:, 0:qn * 128],
                                 wuT[r0:r0 + 128, c0:c1]).then_inc(wsem, 16)
                    n += 1

    @block.tensor
    def _(t):
        n = 0
        nq = 0
        for h in range(NH):
            for (rank, q0, qn) in quads:
                for dc in range(32):
                    t.wait_ge(wld if n % 2 == 0 else wld1,
                              (WB0 if n % 2 == 0 else WB1) + 32 * (n // 2 + 1))
                    if dc == 0 and nq >= 1:
                        t.wait_ge(qev, QV + nq)
                    first, last = dc == 0, dc == 31
                    for k in range(qn):
                        t.matmul(ctx.ps[k][:, :], ctx.wa[:, n % 2, k * 128:(k + 1) * 128],
                                 ctx.xlnb[:, dc, h * 512:(h + 1) * 512], start=first, stop=last)
                        ins = t.matmul(ctx.ps[3 + k][:, :],
                                       ctx.wb[:, n % 2, :, :].rearrange("p a b -> p (a b)")[:, k * 128:(k + 1) * 128],
                                       ctx.xlnb[:, dc, h * 512:(h + 1) * 512], start=first, stop=last)
                    ins.then_inc(mmu, 1)
                    n += 1
                nq += 1

    allq = [q for h in range(NH) for q in quads]

    @block.scalar
    def _(s):
        smf = ctx.sm[:, :, :].rearrange("p a b -> p (a b)")
        nst = 0
        for nq, (rank, q0, qn) in enumerate(allq):
            s.wait_ge(mmu, MB + 32 * (nq + 1))
            if nq >= 2:
                s.wait_ge(qev, QV + nq - 1)
            s.wait_ge(zst, ZS + 16 * nst)
            base = (nq % 2) * 2048
            for k in range(qn):
                s.activation(out=smf[:, base + k * 512:base + (k + 1) * 512],
                             in_=ctx.ps[k][:, :], func=AF.Silu).then_inc(sil, 1)
            if nq >= 1:
                prank, pq0, pqn = allq[nq - 1]
                ph = (nq - 1) // len(quads)
                s.wait_ge(qev, QV + nq)
                for k in range(pqn):
                    ic = prank * 11 + pq0 + k
                    s.dma_start(z_ap[ic * 128:(ic + 1) * 128, ph * 512:(ph + 1) * 512],
                                ctx.zb[:, nst % 8, :]).then_inc(zst, 16)
                    nst += 1
        lrank, lq0, lqn = allq[-1]
        s.wait_ge(qev, QV + len(allq))
        for k in range(lqn):
            ic = lrank * 11 + lq0 + k
            s.dma_start(z_ap[ic * 128:(ic + 1) * 128, (NH - 1) * 512:NH * 512],
                        ctx.zb[:, nst % 8, :]).then_inc(zst, 16)
            nst += 1

    @block.vector
    def _(v):
        nq = 0
        nsil = 0
        nst = 0
        for h in range(NH):
            for (rank, q0, qn) in quads:
                nsil += qn
                v.wait_ge(sil, SL + nsil)
                base = (nq % 2) * 2048
                smf = ctx.sm[:, :, :].rearrange("p a b -> p (a b)")
                for k in range(qn):
                    if nst >= 8:
                        v.wait_ge(zst, ZS + 16 * (nst - 7))
                    ins = v.tensor_tensor(out=ctx.zb[:, nst % 8, :],
                                          in0=smf[:, base + k * 512:base + (k + 1) * 512],
                                          in1=ctx.ps[3 + k][:, :], op=ALU.mult)
                    nst += 1
                ins.then_inc(qev, 1)
                nq += 1

    NZG = 4

    @block.sync
    def _(sy):
        sy.wait_ge(zst, ZS + 16 * 88 * NH)
        n = 0
        for h in range(NH):
            for hq in range(8):
                for zg in range(NZG):
                    for i22 in range(22):
                        ic = zg * 22 + i22
                        rank, lc = divmod(ic, 11)
                        if n >= 2:
                            sy.wait_ge(dcons, DC + n - 1)
                        r0 = rank * ISLP + lc * 128
                        sy.dma_start(ctx.wa[:, n % 2, 0:512],
                                     wdT[r0:r0 + 128, hq * 512:(hq + 1) * 512]).then_inc(
                                         dwl if n % 2 == 0 else dwl1, 16)
                        n += 1

    @block.scalar
    def _(s):
        n = 0
        nev = 0
        for h in range(NH):
            for hq in range(8):
                for zg in range(NZG):
                    if n >= 1:
                        s.wait_ge(dcons, DC + 22 * n)
                    s.dma_start(ctx.zfull[:, :, :],
                                z_ap.rearrange("(a p) c -> p a c", p=128)[:, zg * 22:(zg + 1) * 22,
                                                                          h * 512:(h + 1) * 512]).then_inc(zld, 16)
                    n += 1
                for k in range(4):
                    dcc = hq * 4 + k
                    if nev >= 2:
                        s.wait_ge(dev, DV + nev - 1)
                    s.dma_start(ctx.xf[:, nev % 2, 0:512],
                                x_ap[dcc * 128:(dcc + 1) * 128, h * 512:(h + 1) * 512]).then_inc(
                                    xr if nev % 2 == 0 else xr1, 16)
                    s.wait_ge(dev, DV + nev + 1)
                    s.dma_start(xout_ap[dcc * 128:(dcc + 1) * 128, h * 512:(h + 1) * 512],
                                ctx.sf[:, 4 + nev % 2, :]).then_inc(xw if nev % 2 == 0 else xw1, 16)
                    nev += 1
        s.wait_ge(xw, XW0 + 16 * NH * 16)
        s.wait_ge(xw1, XW1 + 16 * NH * 16)
        s.nop().then_inc(dnh, 1)

    @block.tensor
    def _(t):
        n = 0
        ng = 0
        for h in range(NH):
            for hq in range(8):
                if h + hq > 0:
                    t.wait_ge(dev, DV + (h * 8 + hq) * 4)
                for zg in range(NZG):
                    ng += 1
                    t.wait_ge(zld, ZL + 16 * ng)
                    for i22 in range(22):
                        ic = zg * 22 + i22
                        t.wait_ge(dwl if n % 2 == 0 else dwl1,
                                  (DW0 if n % 2 == 0 else DW1) + 16 * (n // 2 + 1))
                        first = (zg == 0 and i22 == 0)
                        last = (zg == NZG - 1 and i22 == 21)
                        for k in range(4):
                            ins = t.matmul(ctx.ps[k][:, :], ctx.wa[:, n % 2, k * 128:(k + 1) * 128],
                                           ctx.zfull[:, i22, :], start=first, stop=last)
                        ins.then_inc(dcons, 1)
                        n += 1

    @block.vector
    def _(v):
        nev = 0
        for h in range(NH):
            for hq in range(8):
                v.wait_ge(dcons, DC + (h * 8 + hq + 1) * 88)
                for k in range(4):
                    v.wait_ge(xr if nev % 2 == 0 else xr1,
                              (XR0 if nev % 2 == 0 else XR1) + 16 * (nev // 2 + 1))
                    if nev >= 2:
                        v.wait_ge(xw if nev % 2 == 0 else xw1,
                                  (XW0 if nev % 2 == 0 else XW1) + 16 * ((nev - 2) // 2 + 1))
                    v.scalar_tensor_tensor(out=ctx.sf[:, 4 + nev % 2, :],
                                           in0=ctx.ps[k][:, :], scalar=0.5, in1=ctx.xf[:, nev % 2, 0:512],
                                           op0=ALU.mult, op1=ALU.add).then_inc(dev, 1)
                    nev += 1

    return ("f0_dn", DN + 1)


def emit_attn(ctx, block, tag, x_ap, xout_ap, qkvwb, qkvb_rs, vbias_row, owb, ob_rs,
              qT, kT_sl, kT_full, v_sl, v_full, waits=(), wwaits=()):
    ln_done = emit_ln(ctx, block, f"{tag}l", x_ap, waits=waits)
    SCALE = 0.08838834764831845
    bld = ctx.sem(f"{tag}_bld")
    pwl = [ctx.sem(f"{tag}_pwl0"), ctx.sem(f"{tag}_pwl1")]
    pmc = ctx.sem(f"{tag}_pmc")
    pev = ctx.sem(f"{tag}_pev")
    pst = ctx.sem(f"{tag}_pst")
    cc2 = ctx.sem(f"{tag}_cc2")
    akl = [ctx.sem(f"{tag}_akl0"), ctx.sem(f"{tag}_akl1")]
    avl = [ctx.sem(f"{tag}_avl0"), ctx.sem(f"{tag}_avl1")]
    aql = [ctx.sem(f"{tag}_aql0"), ctx.sem(f"{tag}_aql1")]
    amc = ctx.sem(f"{tag}_amc")
    aex = ctx.sem(f"{tag}_aex")
    aep = ctx.sem(f"{tag}_aep")
    opw = [ctx.sem(f"{tag}_opw0"), ctx.sem(f"{tag}_opw1")]
    opc = ctx.sem(f"{tag}_opc")
    odev = ctx.sem(f"{tag}_odev")
    oxr = [ctx.sem(f"{tag}_oxr0"), ctx.sem(f"{tag}_oxr1")]
    oxw = [ctx.sem(f"{tag}_oxw0"), ctx.sem(f"{tag}_oxw1")]

    NT_QK, NT_V = 32, 16
    NT = NT_QK + NT_V

    @block.scalar
    def _(s):
        s.dma_start(ctx.bias1[:, 0:64], qkvb_rs[:, :]).then_inc(bld, 16)
        s.dma_start(ctx.bias1[:, 64:96], vbias_row[:, :]).then_inc(bld, 16)
        s.dma_start(ctx.bias1[:, 96:128], ob_rs[:, :]).then_inc(bld, 16)

    @block.sync
    def _(sy):
        sy.wait_ge(ctx.sem(ln_done[0]), ln_done[1])
        for (wname, wv) in wwaits:
            sy.wait_ge(ctx.sem(wname), wv)
        for t in range(NT):
            for dc in range(32):
                n = t * 32 + dc
                if n >= 2:
                    sy.wait_ge(pmc, n - 1)
                if t < NT_QK:
                    h2, ocg = divmod(t, 16)
                    c0 = ocg * 512
                else:
                    hg, tcg = divmod(t - NT_QK, 2)
                    c0 = 8192 + hg * 512
                sy.dma_start(ctx.wa[:, n % 2, 0:512],
                             qkvwb[dc * 128:(dc + 1) * 128, c0:c0 + 512]).then_inc(pwl[n % 2], 16)

    @block.tensor
    def _(t_):
        for t in range(NT):
            if t >= 1:
                t_.wait_ge(pev, t)
            for dc in range(32):
                n = t * 32 + dc
                t_.wait_ge(pwl[n % 2], 16 * (n // 2 + 1))
                first, last = dc == 0, dc == 31
                for k in range(4):
                    if t < NT_QK:
                        h2 = t // 16
                        ins = t_.matmul(ctx.ps[k][:, :], ctx.wa[:, n % 2, k * 128:(k + 1) * 128],
                                        ctx.xlnb[:, dc, h2 * 512:(h2 + 1) * 512],
                                        start=first, stop=last)
                    else:
                        hg, tcg = divmod(t - NT_QK, 2)
                        tc = tcg * 4 + k
                        ins = t_.matmul(ctx.ps[k][:, :], ctx.xlnb[:, dc, tc * 128:(tc + 1) * 128],
                                        ctx.wa[:, n % 2, 0:512], start=first, stop=last)
                ins.then_inc(pmc, 1)

    @block.vector
    def _(v):
        nst = 0
        for t in range(NT):
            v.wait_ge(pmc, 32 * (t + 1))
            v.wait_ge(bld, 48)
            for k in range(4):
                if nst >= 8:
                    v.wait_ge(pst, 16 * (nst - 7))
                if t < NT_QK:
                    h2, ocg = divmod(t, 16)
                    oc = ocg * 4 + k
                    ins = v.tensor_scalar(out=ctx.zb[:, nst % 8, :], in0=ctx.ps[k][:, :],
                                          scalar1=ctx.bias1[:, oc:oc + 1], scalar2=None,
                                          op0=ALU.add)
                else:
                    ins = v.tensor_copy(ctx.zb[:, nst % 8, :], ctx.ps[k][:, :])
                nst += 1
            ins.then_inc(pev, 1)

    @block.scalar
    def _(s):
        nst = 0
        for t in range(NT):
            s.wait_ge(pev, t + 1)
            s.wait_ge(pst, 16 * nst)
            for k in range(4):
                if t < NT_QK:
                    h2, ocg = divmod(t, 16)
                    oc = ocg * 4 + k
                    if oc < 32:
                        dst = qT[oc * 128:(oc + 1) * 128, h2 * 512:(h2 + 1) * 512]
                    else:
                        dst = kT_sl[(oc - 32) * 128:(oc - 31) * 128, h2 * 512:(h2 + 1) * 512]
                else:
                    hg, tcg = divmod(t - NT_QK, 2)
                    tc = tcg * 4 + k
                    dst = v_sl[hg * 1024 + tc * 128:hg * 1024 + (tc + 1) * 128, :]
                s.dma_start(dst, ctx.zb[:, nst % 8, :]).then_inc(pst, 16)
                nst += 1

    @block.gpsimd
    def _(g):
        g.wait_ge(pst, 16 * 4 * NT)
        g.collective_compute("AllGather", ALU.bypass, replica_groups=RG,
                             ins=[kT_sl.opt()], outs=[kT_full.opt()]).then_inc(cc2)
        g.collective_compute("AllGather", ALU.bypass, replica_groups=RG,
                             ins=[v_sl.opt()], outs=[v_full.opt()]).then_inc(cc2)
        g.wait_ge(cc2, 2)

    @block.sync
    def _(sy):
        sy.wait_ge(cc2, 2)
        n = 0
        for h in range(32):
            hg, hin = divmod(h, 4)
            for rank in range(8):
                if n >= 2:
                    sy.wait_ge(amc, (n - 1) * 8)
                sy.dma_start(ctx.wa[:, n % 2, :],
                             kT_full[rank * 4096 + h * 128:rank * 4096 + (h + 1) * 128, :]).then_inc(akl[n % 2], 16)
                sy.dma_start(ctx.wb[:, n % 2, :, :],
                             v_full.rearrange("(x p) c -> p x c", p=128)[:, rank * 64 + hg * 8:rank * 64 + hg * 8 + 8,
                                                                         hin * 128:(hin + 1) * 128]).then_inc(avl[n % 2], 16)
                n += 1

    @block.scalar
    def _(s):
        K = 0
        for h in range(32):
            if h >= 2:
                s.wait_ge(aep, h - 1)
            s.dma_start(ctx.sm[:, h % 2, :], qT[h * 128:(h + 1) * 128, :]).then_inc(aql[h % 2], 16)
            for kvc in range(64):
                for qb in range(2):
                    if K >= 2:
                        s.wait_ge(amc, K - 1)
                    s.wait_ge(ctx.sem(f"{tag}_asc"), 2 * K + qb + 1)
                    s.activation(out=ctx.zb[:, 4 + (K % 2) * 2 + qb, :],
                                 in_=ctx.ps[3 + qb * 2 + (K % 2)][:, :],
                                 func=AF.Exp, scale=SCALE).then_inc(aex, 1)
                K += 1

    @block.tensor
    def _(t_):
        asc = ctx.sem(f"{tag}_asc")

        def emit_scores(K):
            if K >= 2:
                t_.wait_ge(aex, 2 * K - 2)
            h, j = divmod(K, 64)
            piece = h * 8 + j // 8
            lc2 = j % 8
            for qb in range(2):
                t_.matmul(ctx.ps[3 + qb * 2 + (K % 2)][:, :],
                          ctx.wa[:, piece % 2, lc2 * 128:(lc2 + 1) * 128],
                          ctx.sm[:, h % 2, qb * 512:(qb + 1) * 512],
                          start=True, stop=True).then_inc(asc, 1)

        for h in range(32):
            t_.wait_ge(aql[h % 2], 16 * (h // 2 + 1))
            if h >= 1:
                t_.wait_ge(aep, h)
            for rank in range(8):
                piece = h * 8 + rank
                t_.wait_ge(akl[piece % 2], 16 * (piece // 2 + 1))
                t_.wait_ge(avl[piece % 2], 16 * (piece // 2 + 1))
                for lc2 in range(8):
                    K = h * 64 + rank * 8 + lc2
                    if rank == 0 and lc2 == 0:
                        emit_scores(K)
                    if rank * 8 + lc2 < 63:
                        if lc2 == 7:
                            t_.wait_ge(akl[(piece + 1) % 2], 16 * ((piece + 1) // 2 + 1))
                        emit_scores(K + 1)
                    t_.wait_ge(aex, 2 * K + 2)
                    first = rank == 0 and lc2 == 0
                    last = rank == 7 and lc2 == 7
                    for qb in range(2):
                        t_.matmul(ctx.ps[qb][:, :], ctx.wb[:, piece % 2, lc2, :],
                                  ctx.zb[:, 4 + (K % 2) * 2 + qb, :], start=first, stop=last)
                        ins = t_.matmul(ctx.ps[2][(qb * 32):(qb * 32 + 1), :], ctx.onesc[:, 0:1],
                                        ctx.zb[:, 4 + (K % 2) * 2 + qb, :], start=first, stop=last)
                    ins.then_inc(amc, 1)
            t_.wait_ge(ctx.sem(f"{tag}_aev2"), h + 1)
            for qb in range(2):
                t_.matmul(ctx.ps[3 + qb * 2][:, :], ctx.ones1[0:1, :],
                          ctx.rows[0:1, (h % 2) * 1024 + qb * 512:(h % 2) * 1024 + (qb + 1) * 512],
                          start=True, stop=True).then_inc(ctx.sem(f"{tag}_abc"), 1)

    aev = ctx.sem(f"{tag}_aev")
    aev2 = ctx.sem(f"{tag}_aev2")
    abc = ctx.sem(f"{tag}_abc")

    @block.vector
    def _(v):
        for h in range(32):
            v.wait_ge(amc, (h + 1) * 64)
            for qb in range(2):
                v.tensor_copy(ctx.rows[0:1, (h % 2) * 1024 + qb * 512:(h % 2) * 1024 + (qb + 1) * 512],
                              ctx.ps[2][qb * 32:qb * 32 + 1, :]).then_inc(aev, 1)
            v.reciprocal(ctx.rows[0:1, (h % 2) * 1024:(h % 2) * 1024 + 1024],
                         ctx.rows[0:1, (h % 2) * 1024:(h % 2) * 1024 + 1024]).then_inc(aev2, 1)
            v.wait_ge(abc, 2 * (h + 1))
            for qb in range(2):
                v.tensor_copy(ctx.sf[:, 4 + qb, :], ctx.ps[qb][:, :])
                v.tensor_tensor(out=ctx.sf[:, 6 + qb, :],
                                in0=ctx.sf[:, 4 + qb, :], in1=ctx.ps[3 + qb * 2][:, :],
                                op=ALU.mult)
                ins = v.tensor_scalar(out=ctx.xlnb[:, h, qb * 512:(qb + 1) * 512],
                                      in0=ctx.sf[:, 6 + qb, :],
                                      scalar1=ctx.bias1[:, 64 + h:65 + h], scalar2=None,
                                      op0=ALU.add)
            ins.then_inc(aep, 1)

    @block.sync
    def _(sy):
        sy.wait_ge(aep, 32)
        n = 0
        for h2 in range(2):
            for oq in range(8):
                for hc in range(32):
                    if n >= 2:
                        sy.wait_ge(opc, n - 1)
                    sy.dma_start(ctx.wa[:, n % 2, 0:512],
                                 owb[hc * 128:(hc + 1) * 128, oq * 512:(oq + 1) * 512]).then_inc(opw[n % 2], 16)
                    n += 1

    @block.tensor
    def _(t_):
        n = 0
        for h2 in range(2):
            for oq in range(8):
                g = h2 * 8 + oq
                if g >= 1:
                    t_.wait_ge(odev, g * 4)
                for hc in range(32):
                    t_.wait_ge(opw[n % 2], 16 * (n // 2 + 1))
                    first, last = hc == 0, hc == 31
                    for k in range(4):
                        ins = t_.matmul(ctx.ps[k][:, :], ctx.wa[:, n % 2, k * 128:(k + 1) * 128],
                                        ctx.xlnb[:, hc, h2 * 512:(h2 + 1) * 512], start=first, stop=last)
                    ins.then_inc(opc, 1)
                    n += 1

    @block.vector
    def _(v):
        nev = 0
        for h2 in range(2):
            for oq in range(8):
                v.wait_ge(opc, (h2 * 8 + oq + 1) * 32)
                for k in range(4):
                    v.wait_ge(oxr[nev % 2], 16 * (nev // 2 + 1))
                    if nev >= 2:
                        v.wait_ge(oxw[nev % 2], 16 * ((nev - 2) // 2 + 1))
                    oc2 = oq * 4 + k
                    v.scalar_tensor_tensor(out=ctx.sf[:, 4 + nev % 2, :], in0=ctx.ps[k][:, :],
                                           scalar=ctx.bias1[:, 96 + oc2:97 + oc2],
                                           in1=ctx.xf[:, nev % 2, 0:512],
                                           op0=ALU.add, op1=ALU.add).then_inc(odev, 1)
                    nev += 1

    @block.scalar
    def _(s):
        nev = 0
        for h2 in range(2):
            for oq in range(8):
                for k in range(4):
                    dcc = oq * 4 + k
                    if nev >= 2:
                        s.wait_ge(odev, nev - 1)
                    s.dma_start(ctx.xf[:, nev % 2, 0:512],
                                x_ap[dcc * 128:(dcc + 1) * 128, h2 * 512:(h2 + 1) * 512]).then_inc(oxr[nev % 2], 16)
                    s.wait_ge(odev, nev + 1)
                    s.dma_start(xout_ap[dcc * 128:(dcc + 1) * 128, h2 * 512:(h2 + 1) * 512],
                                ctx.sf[:, 4 + nev % 2, :]).then_inc(oxw[nev % 2], 16)
                    nev += 1
        s.wait_ge(oxw[0], 16 * 32)
        s.wait_ge(oxw[1], 16 * 32)
        s.nop().then_inc(ctx.sem(f"{tag}_dn"), 1)
    return (f"{tag}_dn", 1)


def emit_ffn1(ctx, block, tag, x_ap, xout_ap, w1qT, scale_rs, bias_rs, waits=(), wwaits=()):
    ln_done = emit_ln(ctx, block, f"{tag}l", x_ap, waits=waits)
    bld, BL = ctx.sem2("f1_bld", 32)
    hsd, HS = ctx.sem2("f1_hsd", 1)
    fwl0, FW0 = ctx.sem2("f1_fwl0", 4096)
    fwl1, FW1 = ctx.sem2("f1_fwl1", 4096)
    fmc, FM = ctx.sem2("f1_fmc", 512)
    fdev, FD = ctx.sem2("f1_fdev", 64)
    fxr0, FR0 = ctx.sem2("f1_fxr0", 512)
    fxr1, FR1 = ctx.sem2("f1_fxr1", 512)
    fxw0, FWW0 = ctx.sem2("f1_fxw0", 512)
    fxw1, FWW1 = ctx.sem2("f1_fxw1", 512)
    dnh, DN = ctx.sem2("f1_dn", 1)

    @block.scalar
    def _(s):
        s.dma_start(ctx.bias1[:, 0:32], scale_rs[:, :]).then_inc(bld, 16)
        s.dma_start(ctx.bias1[:, 32:64], bias_rs[:, :]).then_inc(bld, 16)

    @block.vector
    def _(v):
        v.wait_ge(bld, BL + 32)
        v.tensor_scalar(out=ctx.bias1[:, 0:32], in0=ctx.bias1[:, 0:32], scalar1=0.5,
                        scalar2=None, op0=ALU.mult)
        v.tensor_scalar(out=ctx.bias1[:, 32:64], in0=ctx.bias1[:, 32:64], scalar1=0.5,
                        scalar2=None, op0=ALU.mult).then_inc(hsd, 1)

    @block.sync
    def _(sy):
        sy.wait_ge(ctx.sem(ln_done[0]), ln_done[1])
        for (wname, wv) in wwaits:
            sy.wait_ge(ctx.sem(wname), wv)
        n = 0
        for h2 in range(2):
            for og in range(8):
                for dc in range(32):
                    if n >= 2:
                        sy.wait_ge(fmc, FM + n - 1)
                    sy.dma_start(ctx.wa[:, n % 2, 0:512],
                                 w1qT[dc * 128:(dc + 1) * 128, og * 512:(og + 1) * 512]).then_inc(
                                     fwl0 if n % 2 == 0 else fwl1, 16)
                    n += 1

    @block.tensor
    def _(t_):
        n = 0
        for h2 in range(2):
            for og in range(8):
                g = h2 * 8 + og
                if g >= 1:
                    t_.wait_ge(fdev, FD + g * 4)
                for dc in range(32):
                    t_.wait_ge(fwl0 if n % 2 == 0 else fwl1,
                               (FW0 if n % 2 == 0 else FW1) + 16 * (n // 2 + 1))
                    first, last = dc == 0, dc == 31
                    for k in range(4):
                        ins = t_.matmul(ctx.ps[k][:, :], ctx.wa[:, n % 2, k * 128:(k + 1) * 128],
                                        ctx.xlnb[:, dc, h2 * 512:(h2 + 1) * 512], start=first, stop=last)
                    ins.then_inc(fmc, 1)
                    n += 1

    @block.vector
    def _(v):
        v.wait_ge(hsd, HS + 1)
        nev = 0
        for h2 in range(2):
            for og in range(8):
                v.wait_ge(fmc, FM + (h2 * 8 + og + 1) * 32)
                for k in range(4):
                    oc = og * 4 + k
                    v.wait_ge(fxr0 if nev % 2 == 0 else fxr1,
                              (FR0 if nev % 2 == 0 else FR1) + 16 * (nev // 2 + 1))
                    if nev >= 2:
                        v.wait_ge(fxw0 if nev % 2 == 0 else fxw1,
                                  (FWW0 if nev % 2 == 0 else FWW1) + 16 * ((nev - 2) // 2 + 1))
                    v.tensor_scalar(out=ctx.sf[:, 6 + nev % 2, :], in0=ctx.ps[k][:, :],
                                    scalar1=ctx.bias1[:, oc:oc + 1],
                                    scalar2=ctx.bias1[:, 32 + oc:33 + oc],
                                    op0=ALU.mult, op1=ALU.add)
                    v.tensor_tensor(out=ctx.sf[:, 4 + nev % 2, :], in0=ctx.sf[:, 6 + nev % 2, :],
                                    in1=ctx.xf[:, nev % 2, 0:512], op=ALU.add).then_inc(fdev, 1)
                    nev += 1

    @block.scalar
    def _(s):
        nev = 0
        for h2 in range(2):
            for og in range(8):
                for k in range(4):
                    dcc = og * 4 + k
                    if nev >= 2:
                        s.wait_ge(fdev, FD + nev - 1)
                    s.dma_start(ctx.xf[:, nev % 2, 0:512],
                                x_ap[dcc * 128:(dcc + 1) * 128, h2 * 512:(h2 + 1) * 512]).then_inc(
                                    fxr0 if nev % 2 == 0 else fxr1, 16)
                    s.wait_ge(fdev, FD + nev + 1)
                    s.dma_start(xout_ap[dcc * 128:(dcc + 1) * 128, h2 * 512:(h2 + 1) * 512],
                                ctx.sf[:, 4 + nev % 2, :]).then_inc(fxw0 if nev % 2 == 0 else fxw1, 16)
                    nev += 1
        s.wait_ge(fxw0, FWW0 + 16 * 32)
        s.wait_ge(fxw1, FWW1 + 16 * 32)
        s.nop().then_inc(dnh, 1)
    return ("f1_dn", DN + 1)


def _ln512(ctx, block, dst_sm, wtile, wc0, btile, bc0, silu, waits_v, eps=1e-5):
    c5h, C5 = ctx.sem2("i5_c", 4)
    q5h, Q5 = ctx.sem2("i5_q", 4)
    t5h, T5 = ctx.sem2("i5_st", 4)
    r5h, R5 = ctx.sem2("i5_r", 5)
    a5h, A5 = ctx.sem2("i5_a", 8)
    s5h, S5 = ctx.sem2("i5_s", 8)

    @block.vector
    def _(v):
        for (sh, sv) in waits_v:
            v.wait_ge(sh, sv)
        for cc in range(4):
            v.tensor_copy(dst_sm[:, cc, :], ctx.hb[:, cc, :]).then_inc(c5h, 1)

    @block.scalar
    def _(s):
        for cc in range(4):
            s.wait_ge(c5h, C5 + cc + 1)
            if cc >= 2:
                s.wait_ge(t5h, T5 + cc - 1)
            s.activation(out=ctx.zb[:, 2 * (cc % 2):2 * (cc % 2) + 2, :].rearrange("p a b -> p (a b)"),
                         in_=dst_sm[:, cc, :], func=AF.Square).then_inc(q5h, 1)

    @block.tensor
    def _(t):
        for cc in range(4):
            t.wait_ge(q5h, Q5 + cc + 1)
            first, last = cc == 0, cc == 3
            sqv = ctx.zb[:, 2 * (cc % 2):2 * (cc % 2) + 2, :].rearrange("p a b -> p (a b)")
            for h2 in range(NH):
                t.matmul(ctx.ps[0 + h2][0:1, :], ctx.onesc[:, 0:1],
                         dst_sm[:, cc, h2 * 512:(h2 + 1) * 512], start=first, stop=last)
                ins = t.matmul(ctx.ps[2 + h2][0:1, :], ctx.onesc[:, 0:1],
                               sqv[:, h2 * 512:(h2 + 1) * 512], start=first, stop=last)
            ins.then_inc(t5h, 1)

    @block.vector
    def _(v):
        v.wait_ge(t5h, T5 + 4)
        for h2 in range(NH):
            v.tensor_scalar(out=ctx.rows[0:1, h2 * 512:h2 * 512 + 512],
                            in0=ctx.ps[0 + h2][0:1, :], scalar1=1.0 / 512, scalar2=None, op0=ALU.mult)
            v.tensor_scalar(out=ctx.rows[0:1, 1024 + h2 * 512:1024 + h2 * 512 + 512],
                            in0=ctx.ps[2 + h2][0:1, :], scalar1=1.0 / 512, scalar2=None, op0=ALU.mult)
        v.tensor_tensor(out=ctx.rows[0:1, 2048:3072], in0=ctx.rows[0:1, 0:1024],
                        in1=ctx.rows[0:1, 0:1024], op=ALU.mult)
        v.tensor_tensor(out=ctx.rows[0:1, 2048:3072], in0=ctx.rows[0:1, 1024:2048],
                        in1=ctx.rows[0:1, 2048:3072], op=ALU.subtract)
        v.tensor_scalar(out=ctx.rows[0:1, 2048:3072], in0=ctx.rows[0:1, 2048:3072],
                        scalar1=float(eps), scalar2=None, op0=ALU.add).then_inc(r5h, 1)

    @block.scalar
    def _(s):
        s.wait_ge(r5h, R5 + 1)
        s.activation(out=ctx.rows[0:1, 3072:4096], in_=ctx.rows[0:1, 2048:3072],
                     func=AF.Sqrt).then_inc(r5h, 1)

    @block.vector
    def _(v):
        v.wait_ge(r5h, R5 + 2)
        v.reciprocal(ctx.rows[0:1, 3072:4096], ctx.rows[0:1, 3072:4096]).then_inc(r5h, 1)

    @block.tensor
    def _(t):
        t.wait_ge(r5h, R5 + 3)
        for h2 in range(NH):
            t.matmul(ctx.ps[0 + h2][:, :], ctx.ones1[0:1, :],
                     ctx.rows[0:1, h2 * 512:h2 * 512 + 512], start=True, stop=True)
            ins = t.matmul(ctx.ps[2 + h2][:, :], ctx.ones1[0:1, :],
                           ctx.rows[0:1, 3072 + h2 * 512:3072 + h2 * 512 + 512],
                           start=True, stop=True)
        ins.then_inc(r5h, 2)

    @block.vector
    def _(v):
        v.wait_ge(r5h, R5 + 5)
        k = 0
        for cc in range(4):
            for h2 in range(NH):
                v.tensor_tensor(out=ctx.sf[:, 0, :], in0=dst_sm[:, cc, h2 * 512:(h2 + 1) * 512],
                                in1=ctx.ps[0 + h2][:, :], op=ALU.subtract)
                v.tensor_tensor(out=ctx.sf[:, 1, :], in0=ctx.sf[:, 0, :],
                                in1=ctx.ps[2 + h2][:, :], op=ALU.mult)
                if silu:
                    if k >= 2:
                        v.wait_ge(s5h, S5 + k - 1)
                    v.tensor_scalar(out=ctx.sf[:, 2 + k % 2, :], in0=ctx.sf[:, 1, :],
                                    scalar1=wtile[:, wc0 + cc:wc0 + cc + 1],
                                    scalar2=btile[:, bc0 + cc:bc0 + cc + 1],
                                    op0=ALU.mult, op1=ALU.add).then_inc(a5h, 1)
                else:
                    v.tensor_scalar(out=dst_sm[:, cc, h2 * 512:(h2 + 1) * 512], in0=ctx.sf[:, 1, :],
                                    scalar1=wtile[:, wc0 + cc:wc0 + cc + 1],
                                    scalar2=btile[:, bc0 + cc:bc0 + cc + 1],
                                    op0=ALU.mult, op1=ALU.add).then_inc(a5h, 1)
                k += 1

    if silu:
        @block.scalar
        def _(s):
            k = 0
            for cc in range(4):
                for h2 in range(NH):
                    s.wait_ge(a5h, A5 + k + 1)
                    s.activation(out=dst_sm[:, cc, h2 * 512:(h2 + 1) * 512],
                                 in_=ctx.sf[:, 2 + k % 2, :], func=AF.Silu).then_inc(s5h, 1)
                    k += 1
        return (s5h, S5 + 8)

    @block.scalar
    def _(s):
        s.wait_ge(a5h, A5 + 8)
        s.nop().then_inc(s5h, 8)
    return (a5h, A5 + 8)


def emit_l1attn(ctx, block, tag, x_ap, xout_ap, winT, w1T, w2T, woutT, prm,
                agA_in, agA_out, waits=(), wwaits=()):
    ln_done = emit_ln(ctx, block, f"{tag}l", x_ap, waits=waits)
    B2 = ctx.bias2
    pld = ctx.sem(f"{tag}_pld")
    stp = ctx.sem(f"{tag}_stp")
    drv = ctx.sem(f"{tag}_drv")
    rbd = ctx.sem(f"{tag}_rbd")
    wwl = [ctx.sem(f"{tag}_wwl0"), ctx.sem(f"{tag}_wwl1")]
    wmc = ctx.sem(f"{tag}_wmc")
    pgd = ctx.sem(f"{tag}_pgd")
    gev = ctx.sem(f"{tag}_gev")
    oev = ctx.sem(f"{tag}_oev")
    ast = ctx.sem(f"{tag}_ast")
    ald = ctx.sem(f"{tag}_ald")
    agc = ctx.sem(f"{tag}_agc")
    sxr = [ctx.sem(f"{tag}_sxr0"), ctx.sem(f"{tag}_sxr1")]
    sxw = [ctx.sem(f"{tag}_sxw0"), ctx.sem(f"{tag}_sxw1")]
    dn5 = ctx.sem(f"{tag}_dn")

    X1 = ctx.xf[:, 0, 512:1024]
    X2 = ctx.xf[:, 1, 0:512]
    X3 = ctx.xf[:, 1, 512:1024]

    (C_LN1W, C_LN1B, C_LN2W, C_LN2B, C_SCL, C_PZR, C_PZI, C_LCR, C_LCI, C_B1, C_B2v,
     C_BIN, C_LNLW, C_LNLB, C_BOUT, C_MOH, C_RHO, C_C1, C_S1, C_RHOC, C_RPOW, C_EKC,
     C_EKS, C_E10C, C_E10S, C_CRE, C_CIM, C_SELR, C_SELI, C_T1, C_T2, C_T3, C_T4,
     C_W1) = (0, 16, 32, 48, 64, 80, 96, 112, 128, 144, 208, 224, 228, 232, 236, 268,
              280, 284, 288, 292, 296, 336, 376, 416, 420, 424, 428, 432, 436, 440,
              444, 448, 452, 456)

    def SP_scv(d, cc):
        return 5 * d + cc + 1

    def SP_cor(d):
        return 5 * d + 5

    PL = [("ln1w", C_LN1W, 16), ("ln1b", C_LN1B, 16), ("ln2w", C_LN2W, 16),
          ("ln2b", C_LN2B, 16), ("scl", C_SCL, 16), ("pzr", C_PZR, 16), ("pzi", C_PZI, 16),
          ("lcr", C_LCR, 16), ("lci", C_LCI, 16), ("b1", C_B1, 64), ("b2", C_B2v, 16),
          ("bin", C_BIN, 4), ("lnlw", C_LNLW, 4), ("lnlb", C_LNLB, 4), ("bout", C_BOUT, 32),
          ("maskoh", C_MOH, 8)]
    NPL = 16 * len(PL)

    @block.scalar
    def _(s):
        for name, c0, w in PL:
            s.dma_start(B2[:, c0:c0 + w], prm[name][:, :]).then_inc(pld, 16)

    wload = []
    for h2 in range(NH):
        for dc in range(32):
            wload.append((winT, dc * 128, 0))
    for d in range(4):
        for h2 in range(NH):
            for og in range(4):
                for dc in range(4):
                    wload.append((w1T, d * 512 + dc * 128, og * 512))
            for zc in range(16):
                wload.append((w2T, d * 2048 + zc * 128, 0))
    for h2 in range(NH):
        for og in range(8):
            for dc in range(4):
                wload.append((woutT, dc * 128, og * 512))

    @block.sync
    def _(sy):
        sy.wait_ge(ctx.sem(ln_done[0]), ln_done[1])
        for (wname, wv) in wwaits:
            sy.wait_ge(ctx.sem(wname), wv)
        for n, (src, r0, c0) in enumerate(wload):
            if n >= 2:
                sy.wait_ge(wmc, n - 1)
            sy.dma_start(ctx.wa[:, n % 2, 0:512], src[r0:r0 + 128, c0:c0 + 512]).then_inc(wwl[n % 2], 16)

    @block.tensor
    def _(t):
        t.wait_ge(ctx.sem(ln_done[0]), ln_done[1])
        n = 0
        for h2 in range(NH):
            if h2 == 1:
                t.wait_ge(gev, 1)
            for dc in range(32):
                t.wait_ge(wwl[n % 2], 16 * (n // 2 + 1))
                first, last = dc == 0, dc == 31
                for k in range(4):
                    ins = t.matmul(ctx.ps[k][:, :], ctx.wa[:, n % 2, k * 128:(k + 1) * 128],
                                   ctx.xlnb[:, dc, h2 * 512:(h2 + 1) * 512], start=first, stop=last)
                ins.then_inc(wmc, 1)
                n += 1

    @block.vector
    def _(v):
        v.wait_ge(pld, NPL)
        for h2 in range(NH):
            v.wait_ge(wmc, 32 * (h2 + 1))
            for k in range(4):
                ins = v.tensor_scalar(out=ctx.hb[:, k, h2 * 512:(h2 + 1) * 512], in0=ctx.ps[k][:, :],
                                      scalar1=B2[:, C_BIN + k:C_BIN + k + 1], scalar2=None, op0=ALU.add)
            ins.then_inc(gev, 1)

    wtick = [64]
    gcnt = [2]

    def emit_tables(v0, cc, ct, st_, tp1):
        v = DrainV(v0)
        v.memset(ct[:, 0:1], 1.0)
        v.memset(st_[:, 0:1], 0.0)
        for k in range(9):
            sp = 1 << k
            ekc = B2[:, C_EKC + 4 * k + cc:C_EKC + 4 * k + cc + 1]
            eks = B2[:, C_EKS + 4 * k + cc:C_EKS + 4 * k + cc + 1]
            v.tensor_scalar(out=ct[:, sp:2 * sp], in0=ct[:, 0:sp], scalar1=ekc,
                            scalar2=None, op0=ALU.mult)
            v.tensor_scalar(out=tp1[:, 0:sp], in0=st_[:, 0:sp], scalar1=eks,
                            scalar2=None, op0=ALU.mult)
            v.tensor_tensor(out=ct[:, sp:2 * sp], in0=ct[:, sp:2 * sp],
                            in1=tp1[:, 0:sp], op=ALU.subtract)
            v.tensor_scalar(out=st_[:, sp:2 * sp], in0=st_[:, 0:sp], scalar1=ekc,
                            scalar2=None, op0=ALU.mult)
            v.tensor_scalar(out=tp1[:, 0:sp], in0=ct[:, 0:sp], scalar1=eks,
                            scalar2=None, op0=ALU.mult)
            v.tensor_tensor(out=st_[:, sp:2 * sp], in0=st_[:, sp:2 * sp],
                            in1=tp1[:, 0:sp], op=ALU.add)

    def emit_rotate(v0, cc, ct, st_, tp1, tp2a, tp2b):
        v = DrainV(v0)
        e9c = B2[:, C_EKC + 36 + cc:C_EKC + 36 + cc + 1]
        e9s = B2[:, C_EKS + 36 + cc:C_EKS + 36 + cc + 1]
        v.tensor_scalar(out=tp2a[:, :], in0=ct[:, :], scalar1=e9c, scalar2=None, op0=ALU.mult)
        v.tensor_scalar(out=tp2b[:, :], in0=st_[:, :], scalar1=e9s, scalar2=None, op0=ALU.mult)
        v.tensor_scalar(out=tp1[:, :], in0=ct[:, :], scalar1=e9s, scalar2=None, op0=ALU.mult)
        v.tensor_scalar(out=st_[:, :], in0=st_[:, :], scalar1=e9c, scalar2=None, op0=ALU.mult)
        v.tensor_tensor(out=ct[:, :], in0=tp2a[:, :], in1=tp2b[:, :], op=ALU.subtract)
        v.tensor_tensor(out=st_[:, :], in0=st_[:, :], in1=tp1[:, :], op=ALU.add)

    for d in range(4):
        wv = [(gev, 2)] if d == 0 else [(stp, SP_cor(d - 1))]
        dn1 = _ln512(ctx, block, ctx.sm, B2, C_LN1W + d * 4, B2, C_LN1B + d * 4, True, wv)

        @block.vector
        def _(v, d=d):
            v = DrainV(v)
            v.tensor_tensor(out=B2[:, C_T1:C_T1 + 4], in0=B2[:, C_PZR + d * 4:C_PZR + d * 4 + 4],
                            in1=B2[:, C_PZR + d * 4:C_PZR + d * 4 + 4], op=ALU.mult)
            v.tensor_tensor(out=B2[:, C_T2:C_T2 + 4], in0=B2[:, C_PZI + d * 4:C_PZI + d * 4 + 4],
                            in1=B2[:, C_PZI + d * 4:C_PZI + d * 4 + 4], op=ALU.mult)
            v.tensor_tensor(out=B2[:, C_T1:C_T1 + 4], in0=B2[:, C_T1:C_T1 + 4],
                            in1=B2[:, C_T2:C_T2 + 4], op=ALU.add).then_inc(drv, 1)

        @block.scalar
        def _(s, d=d):
            s.wait_ge(drv, 5 * d + 1)
            s.activation(out=B2[:, C_T2:C_T2 + 4], in_=B2[:, C_T1:C_T1 + 4],
                         func=AF.Sqrt).then_inc(drv, 1)

        @block.vector
        def _(v, d=d):
            v.wait_ge(drv, 5 * d + 2)
            v = DrainV(v)
            v.reciprocal(B2[:, C_T3:C_T3 + 4], B2[:, C_T2:C_T2 + 4])
            v.tensor_tensor(out=B2[:, C_C1:C_C1 + 4], in0=B2[:, C_PZR + d * 4:C_PZR + d * 4 + 4],
                            in1=B2[:, C_T3:C_T3 + 4], op=ALU.mult)
            v.tensor_tensor(out=B2[:, C_S1:C_S1 + 4], in0=B2[:, C_PZI + d * 4:C_PZI + d * 4 + 4],
                            in1=B2[:, C_T3:C_T3 + 4], op=ALU.mult).then_inc(drv, 1)

        @block.scalar
        def _(s, d=d):
            s.wait_ge(drv, 5 * d + 3)
            s.activation(out=B2[:, C_RHO:C_RHO + 4], in_=B2[:, C_T2:C_T2 + 4],
                         func=AF.Tanh).then_inc(drv, 1)

        @block.vector
        def _(v, d=d):
            v.wait_ge(drv, 5 * d + 4)
            v = DrainV(v)
            v.tensor_scalar(out=B2[:, C_RPOW:C_RPOW + 4], in0=B2[:, C_RHO:C_RHO + 4],
                            scalar1=1.0, scalar2=None, op0=ALU.mult)
            for k in range(1, 10):
                v.tensor_tensor(out=B2[:, C_RPOW + 4 * k:C_RPOW + 4 * k + 4],
                                in0=B2[:, C_RPOW + 4 * (k - 1):C_RPOW + 4 * (k - 1) + 4],
                                in1=B2[:, C_RPOW + 4 * (k - 1):C_RPOW + 4 * (k - 1) + 4], op=ALU.mult)
            v.tensor_tensor(out=B2[:, C_RHOC:C_RHOC + 4], in0=B2[:, C_RPOW + 36:C_RPOW + 40],
                            in1=B2[:, C_RPOW + 36:C_RPOW + 40], op=ALU.mult)
            v.tensor_scalar(out=B2[:, C_EKC:C_EKC + 4], in0=B2[:, C_C1:C_C1 + 4],
                            scalar1=1.0, scalar2=None, op0=ALU.mult)
            v.tensor_scalar(out=B2[:, C_EKS:C_EKS + 4], in0=B2[:, C_S1:C_S1 + 4],
                            scalar1=1.0, scalar2=None, op0=ALU.mult)
            for k in range(1, 10):
                pc = B2[:, C_EKC + 4 * (k - 1):C_EKC + 4 * (k - 1) + 4]
                ps_ = B2[:, C_EKS + 4 * (k - 1):C_EKS + 4 * (k - 1) + 4]
                v.tensor_tensor(out=B2[:, C_T1:C_T1 + 4], in0=pc, in1=pc, op=ALU.mult)
                v.tensor_tensor(out=B2[:, C_T2:C_T2 + 4], in0=ps_, in1=ps_, op=ALU.mult)
                v.tensor_tensor(out=B2[:, C_EKC + 4 * k:C_EKC + 4 * k + 4],
                                in0=B2[:, C_T1:C_T1 + 4], in1=B2[:, C_T2:C_T2 + 4], op=ALU.subtract)
                v.tensor_tensor(out=B2[:, C_T3:C_T3 + 4], in0=pc, in1=ps_, op=ALU.mult)
                v.tensor_scalar(out=B2[:, C_EKS + 4 * k:C_EKS + 4 * k + 4],
                                in0=B2[:, C_T3:C_T3 + 4], scalar1=2.0, scalar2=None, op0=ALU.mult)
            p9c = B2[:, C_EKC + 36:C_EKC + 40]
            p9s = B2[:, C_EKS + 36:C_EKS + 40]
            v.tensor_tensor(out=B2[:, C_T1:C_T1 + 4], in0=p9c, in1=p9c, op=ALU.mult)
            v.tensor_tensor(out=B2[:, C_T2:C_T2 + 4], in0=p9s, in1=p9s, op=ALU.mult)
            v.tensor_tensor(out=B2[:, C_E10C:C_E10C + 4], in0=B2[:, C_T1:C_T1 + 4],
                            in1=B2[:, C_T2:C_T2 + 4], op=ALU.subtract)
            v.tensor_tensor(out=B2[:, C_T3:C_T3 + 4], in0=p9c, in1=p9s, op=ALU.mult)
            v.tensor_scalar(out=B2[:, C_E10S:C_E10S + 4], in0=B2[:, C_T3:C_T3 + 4],
                            scalar1=2.0, scalar2=None, op0=ALU.mult).then_inc(drv, 1)

        @block.scalar
        def _(s, d=d):
            s.wait_ge(drv, 5 * d + 4)
            for cc in range(4):
                if cc > 0:
                    s.wait_ge(stp, SP_scv(d, cc - 1))
                s.memzero(ctx.sf[:, 4, :])
                s.activation(out=ctx.sf[:, 4, :], in_=ctx.sf[:, 4, :], func=AF.Relu,
                             scale=0.0, bias=B2[:, C_RHO + cc:C_RHO + cc + 1]).then_inc(rbd, 1)

        @block.vector
        def _(v, d=d, dn1=dn1):
            v.wait_ge(dn1[0], dn1[1])
            v.wait_ge(drv, 5 * d + 5)
            v = DrainV(v)
            ct, st_ = ctx.sf[:, 0, :], ctx.sf[:, 1, :]
            tp1 = ctx.sf[:, 2, :]
            rhob = ctx.sf[:, 4, :]
            u_re, u_im = ctx.sf[:, 5, :], ctx.sf[:, 6, :]
            ym = ctx.zb[:, 0, :]
            for cc in range(4):
                v.wait_ge(rbd, 16 * 0 + 4 * d + cc + 1)
                scl_c = B2[:, C_SCL + d * 4 + cc:C_SCL + d * 4 + cc + 1]
                for hh in range(2):
                    if hh == 0:
                        emit_tables(v, cc, ct, st_, tp1)
                    else:
                        emit_rotate(v, cc, ct, st_, tp1, X1, X2)
                    ysl = ctx.sm[:, cc, hh * 512:(hh + 1) * 512]
                    v.tensor_tensor(out=ym[:, :], in0=ysl, in1=ct[:, :], op=ALU.mult)
                    v.drain()
                    v.tensor_tensor_scan(out=u_re[:, :], data0=rhob[:, :], data1=ym[:, :],
                                         initial=(0.0 if hh == 0 else X3[:, 0:1]),
                                         op0=ALU.mult, op1=ALU.add)
                    v.tensor_tensor(out=ym[:, :], in0=ysl, in1=st_[:, :], op=ALU.mult)
                    v.drain()
                    v.tensor_tensor_scan(out=u_im[:, :], data0=rhob[:, :], data1=ym[:, :],
                                         initial=(0.0 if hh == 0 else X3[:, 1:2]),
                                         op0=ALU.mult, op1=ALU.add)
                    v.drain()
                    if hh == 0:
                        v.tensor_copy(X3[:, 0:1], u_re[:, 511:512])
                        v.tensor_copy(X3[:, 1:2], u_im[:, 511:512])
                    else:
                        v.tensor_copy(X3[:, 8 + cc:9 + cc], u_re[:, 511:512])
                        v.tensor_copy(X3[:, 12 + cc:13 + cc], u_im[:, 511:512])
                    v.tensor_tensor(out=X1[:, :], in0=ct[:, :], in1=u_re[:, :], op=ALU.mult)
                    v.tensor_tensor(out=X2[:, :], in0=st_[:, :], in1=u_im[:, :], op=ALU.mult)
                    v.tensor_tensor(out=X1[:, :], in0=X1[:, :], in1=X2[:, :], op=ALU.add)
                    hsl = ctx.hb[:, cc, hh * 512:(hh + 1) * 512]
                    v.scalar_tensor_tensor(out=X2[:, :], in0=X1[:, :], scalar=scl_c,
                                           in1=hsl, op0=ALU.mult, op1=ALU.add)
                    ins = v.tensor_copy(hsl, X2[:, :])
                ins.then_inc(stp, 1)

        @block.scalar
        def _(s, d=d):
            s.wait_ge(stp, SP_scv(d, 3))
            s.dma_start(agA_in[d].rearrange("(a p) x -> p a x", p=128),
                        X3[:, 8:16].rearrange("p (a x) -> p a x", x=1),
                        allow_slow_non_contiguous=True).then_inc(ast, 16)

        @block.gpsimd
        def _(g, d=d):
            g.wait_ge(ast, 16 * (d + 1))
            g.collective_compute("AllGather", ALU.bypass, replica_groups=RG,
                                 ins=[agA_in[d].opt()],
                                 outs=[agA_out[d].opt()]).then_inc(agc, 1)

        @block.scalar
        def _(s, d=d):
            s.wait_ge(agc, d + 1)
            s.dma_start(ctx.sf[:, 0, 0:64].rearrange("p (j a) -> p j a", j=8),
                        agA_out[d].rearrange("(j a p) x -> p j (a x)", j=8, a=8, p=128),
                        allow_slow_non_contiguous=True).then_inc(ald, 16)

        @block.vector
        def _(v, d=d):
            v.wait_ge(ald, 16 * (d + 1))
            v = DrainV(v)
            A = ctx.sf[:, 0, 0:64]
            c1_ = B2[:, C_C1:C_C1 + 4]
            s1_ = B2[:, C_S1:C_S1 + 4]
            lcr = B2[:, C_LCR + d * 4:C_LCR + d * 4 + 4]
            lci = B2[:, C_LCI + d * 4:C_LCI + d * 4 + 4]
            v.tensor_tensor(out=B2[:, C_T1:C_T1 + 4], in0=c1_, in1=lcr, op=ALU.mult)
            v.tensor_tensor(out=B2[:, C_T2:C_T2 + 4], in0=s1_, in1=lci, op=ALU.mult)
            v.tensor_tensor(out=B2[:, C_CRE:C_CRE + 4], in0=B2[:, C_T1:C_T1 + 4],
                            in1=B2[:, C_T2:C_T2 + 4], op=ALU.subtract)
            v.tensor_tensor(out=B2[:, C_T1:C_T1 + 4], in0=s1_, in1=lcr, op=ALU.mult)
            v.tensor_tensor(out=B2[:, C_T2:C_T2 + 4], in0=c1_, in1=lci, op=ALU.mult)
            v.tensor_tensor(out=B2[:, C_T3:C_T3 + 4], in0=B2[:, C_T1:C_T1 + 4],
                            in1=B2[:, C_T2:C_T2 + 4], op=ALU.add)
            v.tensor_scalar(out=B2[:, C_CIM:C_CIM + 4], in0=B2[:, C_T3:C_T3 + 4],
                            scalar1=-1.0, scalar2=None, op0=ALU.mult)
            v.memset(B2[:, C_SELR:C_SELR + 4], 0.0)
            v.memset(B2[:, C_SELI:C_SELI + 4], 0.0)
            rhoc = B2[:, C_RHOC:C_RHOC + 4]
            e10c = B2[:, C_E10C:C_E10C + 4]
            e10s = B2[:, C_E10S:C_E10S + 4]
            for j in range(8):
                oh = B2[:, C_MOH + j:C_MOH + j + 1]
                v.scalar_tensor_tensor(out=B2[:, C_W1:C_W1 + 4], in0=B2[:, C_CRE:C_CRE + 4],
                                       scalar=oh, in1=B2[:, C_SELR:C_SELR + 4],
                                       op0=ALU.mult, op1=ALU.add)
                v.tensor_copy(B2[:, C_SELR:C_SELR + 4], B2[:, C_W1:C_W1 + 4])
                v.scalar_tensor_tensor(out=B2[:, C_W1:C_W1 + 4], in0=B2[:, C_CIM:C_CIM + 4],
                                       scalar=oh, in1=B2[:, C_SELI:C_SELI + 4],
                                       op0=ALU.mult, op1=ALU.add)
                v.tensor_copy(B2[:, C_SELI:C_SELI + 4], B2[:, C_W1:C_W1 + 4])
                if j < 7:
                    v.tensor_tensor(out=B2[:, C_T1:C_T1 + 4], in0=B2[:, C_CRE:C_CRE + 4],
                                    in1=rhoc, op=ALU.mult)
                    v.tensor_tensor(out=B2[:, C_T1:C_T1 + 4], in0=B2[:, C_T1:C_T1 + 4],
                                    in1=A[:, j * 8:j * 8 + 4], op=ALU.add)
                    v.tensor_tensor(out=B2[:, C_T2:C_T2 + 4], in0=B2[:, C_CIM:C_CIM + 4],
                                    in1=rhoc, op=ALU.mult)
                    v.tensor_tensor(out=B2[:, C_T2:C_T2 + 4], in0=B2[:, C_T2:C_T2 + 4],
                                    in1=A[:, j * 8 + 4:j * 8 + 8], op=ALU.add)
                    v.tensor_tensor(out=B2[:, C_T3:C_T3 + 4], in0=B2[:, C_T1:C_T1 + 4],
                                    in1=e10c, op=ALU.mult)
                    v.tensor_tensor(out=B2[:, C_T4:C_T4 + 4], in0=B2[:, C_T2:C_T2 + 4],
                                    in1=e10s, op=ALU.mult)
                    v.tensor_tensor(out=B2[:, C_CRE:C_CRE + 4], in0=B2[:, C_T3:C_T3 + 4],
                                    in1=B2[:, C_T4:C_T4 + 4], op=ALU.add)
                    v.tensor_tensor(out=B2[:, C_T3:C_T3 + 4], in0=B2[:, C_T2:C_T2 + 4],
                                    in1=e10c, op=ALU.mult)
                    v.tensor_tensor(out=B2[:, C_T4:C_T4 + 4], in0=B2[:, C_T1:C_T1 + 4],
                                    in1=e10s, op=ALU.mult)
                    v.tensor_tensor(out=B2[:, C_CIM:C_CIM + 4], in0=B2[:, C_T3:C_T3 + 4],
                                    in1=B2[:, C_T4:C_T4 + 4], op=ALU.subtract)
            ct, st_ = ctx.sf[:, 1, :], ctx.sf[:, 2, :]
            tp1 = ctx.sf[:, 3, :]
            rp = ctx.sf[:, 7, :]
            for cc in range(4):
                cre_c = B2[:, C_SELR + cc:C_SELR + cc + 1]
                cim_c = B2[:, C_SELI + cc:C_SELI + cc + 1]
                scl_c = B2[:, C_SCL + d * 4 + cc:C_SCL + d * 4 + cc + 1]
                for hh in range(2):
                    if hh == 0:
                        emit_tables(v, cc, ct, st_, tp1)
                        v.tensor_scalar(out=rp[:, 0:1], in0=B2[:, C_RHO + cc:C_RHO + cc + 1],
                                        scalar1=1.0, scalar2=None, op0=ALU.mult)
                        for k in range(9):
                            sp = 1 << k
                            v.tensor_scalar(out=rp[:, sp:2 * sp], in0=rp[:, 0:sp],
                                            scalar1=B2[:, C_RPOW + 4 * k + cc:C_RPOW + 4 * k + cc + 1],
                                            scalar2=None, op0=ALU.mult)
                    else:
                        emit_rotate(v, cc, ct, st_, tp1, X1, X2)
                        v.tensor_scalar(out=rp[:, :], in0=rp[:, :],
                                        scalar1=B2[:, C_RPOW + 36 + cc:C_RPOW + 36 + cc + 1],
                                        scalar2=None, op0=ALU.mult)
                    v.tensor_tensor(out=X1[:, :], in0=ct[:, :], in1=rp[:, :], op=ALU.mult)
                    v.tensor_tensor(out=X2[:, :], in0=st_[:, :], in1=rp[:, :], op=ALU.mult)
                    v.tensor_scalar(out=X1[:, :], in0=X1[:, :], scalar1=cre_c, scalar2=None, op0=ALU.mult)
                    v.tensor_scalar(out=X2[:, :], in0=X2[:, :], scalar1=cim_c, scalar2=None, op0=ALU.mult)
                    v.tensor_tensor(out=X1[:, :], in0=X1[:, :], in1=X2[:, :], op=ALU.add)
                    hsl = ctx.hb[:, cc, hh * 512:(hh + 1) * 512]
                    v.scalar_tensor_tensor(out=X2[:, :], in0=X1[:, :], scalar=scl_c,
                                           in1=hsl, op0=ALU.mult, op1=ALU.add)
                    ins = v.tensor_copy(hsl, X2[:, :])
            ins.then_inc(stp, 1)

        dn2 = _ln512(ctx, block, ctx.sm, B2, C_LN2W + d * 4, B2, C_LN2B + d * 4, False,
                     [(stp, SP_cor(d))])

        WT0 = wtick[0]
        G0 = gcnt[0]

        @block.tensor
        def _(t, d=d, dn2=dn2, WT0=WT0, G0=G0):
            t.wait_ge(dn2[0], dn2[1])
            n = WT0
            g = G0
            for h2 in range(NH):
                for og in range(4):
                    t.wait_ge(gev, g)
                    for dc in range(4):
                        t.wait_ge(wwl[n % 2], 16 * (n // 2 + 1))
                        first, last = dc == 0, dc == 3
                        for k in range(4):
                            ins = t.matmul(ctx.ps[k][:, :], ctx.wa[:, n % 2, k * 128:(k + 1) * 128],
                                           ctx.sm[:, dc, h2 * 512:(h2 + 1) * 512], start=first, stop=last)
                        ins.then_inc(wmc, 1)
                        n += 1
                    g += 1
                t.wait_ge(gev, g)
                for zc in range(16):
                    t.wait_ge(wwl[n % 2], 16 * (n // 2 + 1))
                    first, last = zc == 0, zc == 15
                    for k in range(4):
                        ins = t.matmul(ctx.ps[k][:, :], ctx.wa[:, n % 2, k * 128:(k + 1) * 128],
                                       ctx.zfull[:, zc, :], start=first, stop=last)
                    ins.then_inc(wmc, 1)
                    n += 1
                g += 1

        @block.scalar
        def _(s, d=d, G0=G0, WT0=WT0):
            g = G0
            for h2 in range(NH):
                for og in range(4):
                    s.wait_ge(wmc, WT0 + h2 * 32 + og * 4 + 4)
                    for k in range(4):
                        fc = og * 4 + k
                        ins = s.activation(out=ctx.zfull[:, fc, :], in_=ctx.ps[k][:, :], func=AF.Silu,
                                           bias=B2[:, C_B1 + d * 16 + fc:C_B1 + d * 16 + fc + 1],
                                           scale=1.0)
                    ins.then_inc(gev, 1)
                    g += 1
                g += 1

        @block.vector
        def _(v, d=d, G0=G0, WT0=WT0):
            g = G0
            for h2 in range(NH):
                g += 4
                v.wait_ge(wmc, WT0 + h2 * 32 + 32)
                for k in range(4):
                    v.scalar_tensor_tensor(out=ctx.sf[:, 0, :], in0=ctx.ps[k][:, :],
                                           scalar=B2[:, C_B2v + d * 4 + k:C_B2v + d * 4 + k + 1],
                                           in1=ctx.hb[:, k, h2 * 512:(h2 + 1) * 512],
                                           op0=ALU.add, op1=ALU.add)
                    ins = v.tensor_copy(ctx.hb[:, k, h2 * 512:(h2 + 1) * 512], ctx.sf[:, 0, :])
                ins.then_inc(gev, 1)
                g += 1

        wtick[0] = WT0 + 64
        gcnt[0] = G0 + 10

    dnI = _ln512(ctx, block, ctx.sm, B2, C_LNLW, B2, C_LNLB, False, [])
    WT0 = wtick[0]
    G0 = gcnt[0]

    @block.tensor
    def _(t):
        t.wait_ge(dnI[0], dnI[1])
        n = WT0
        g = G0
        for h2 in range(NH):
            for og in range(8):
                lg = h2 * 8 + og
                if lg == 0:
                    t.wait_ge(gev, G0)
                else:
                    t.wait_ge(oev, 4 * lg)
                for dc in range(4):
                    t.wait_ge(wwl[n % 2], 16 * (n // 2 + 1))
                    first, last = dc == 0, dc == 3
                    for k in range(4):
                        ins = t.matmul(ctx.ps[k][:, :], ctx.wa[:, n % 2, k * 128:(k + 1) * 128],
                                       ctx.sm[:, dc, h2 * 512:(h2 + 1) * 512], start=first, stop=last)
                    ins.then_inc(wmc, 1)
                    n += 1

    @block.vector
    def _(v):
        nev = 0
        for h2 in range(NH):
            for og in range(8):
                v.wait_ge(wmc, WT0 + (h2 * 8 + og + 1) * 4)
                for k in range(4):
                    oc = og * 4 + k
                    v.wait_ge(sxr[nev % 2], 16 * (nev // 2 + 1))
                    if nev >= 2:
                        v.wait_ge(sxw[nev % 2], 16 * ((nev - 2) // 2 + 1))
                    v.scalar_tensor_tensor(out=ctx.sf[:, 2 + nev % 2, :], in0=ctx.ps[k][:, :],
                                           scalar=B2[:, C_BOUT + oc:C_BOUT + oc + 1],
                                           in1=ctx.xf[:, nev % 2, 0:512],
                                           op0=ALU.add, op1=ALU.add).then_inc(oev, 1)
                    nev += 1

    @block.scalar
    def _(s):
        nev = 0
        for h2 in range(NH):
            for og in range(8):
                for k in range(4):
                    dcc = og * 4 + k
                    if nev >= 2:
                        s.wait_ge(oev, nev - 1)
                    s.dma_start(ctx.xf[:, nev % 2, 0:512],
                                x_ap[dcc * 128:(dcc + 1) * 128, h2 * 512:(h2 + 1) * 512]).then_inc(
                                    sxr[nev % 2], 16)
                    s.wait_ge(oev, nev + 1)
                    s.dma_start(xout_ap[dcc * 128:(dcc + 1) * 128, h2 * 512:(h2 + 1) * 512],
                                ctx.sf[:, 2 + nev % 2, :]).then_inc(sxw[nev % 2], 16)
                    nev += 1
        s.wait_ge(sxw[0], 16 * 32)
        s.wait_ge(sxw[1], 16 * 32)
        s.nop().then_inc(dn5, 1)
    return (f"{tag}_dn", 1)



def build_full():
    nc = bass.Bass("TRN2", target_bir_lowering=False, debug=False,
                   detect_race_conditions=False)
    F = F32
    x0 = nc.dram_tensor("x0", [4096, NTOK], F, kind="ExternalInput").ap()
    cb = nc.dram_tensor("cb", [65536, 8], F, kind="ExternalInput").ap()
    idxg = nc.dram_tensor("idxg", [ISLP, 512], U32, kind="ExternalInput").ap()
    idxu = nc.dram_tensor("idxu", [ISLP, 512], U32, kind="ExternalInput").ap()
    idxd = nc.dram_tensor("idxd", [4096, DSL], U32, kind="ExternalInput").ap()
    qkvw = nc.dram_tensor("qkvw", [4096, 12288], F, kind="ExternalInput").ap()
    ow = nc.dram_tensor("ow", [4096, 4096], F, kind="ExternalInput").ap()
    qkvb_rs = nc.dram_tensor("qkvb_rs", [128, 64], F, kind="ExternalInput").ap()
    vb_rs = nc.dram_tensor("vb_rs", [128, 32], F, kind="ExternalInput").ap()
    ob_rs = nc.dram_tensor("ob_rs", [128, 32], F, kind="ExternalInput").ap()
    w1q = nc.dram_tensor("w1q", [4096, 4096], mybir.dt.int8, kind="ExternalInput").ap()
    scale_rs = nc.dram_tensor("scale_rs", [128, 32], F, kind="ExternalInput").ap()
    bias_rs = nc.dram_tensor("bias_rs", [128, 32], F, kind="ExternalInput").ap()
    win = nc.dram_tensor("win", [512, 4096], F, kind="ExternalInput").ap()
    w1 = nc.dram_tensor("w1", [8192, 512], F, kind="ExternalInput").ap()
    w2 = nc.dram_tensor("w2", [2048, 2048], F, kind="ExternalInput").ap()
    wout = nc.dram_tensor("wout", [4096, 512], F, kind="ExternalInput").ap()
    prm = {}
    PRM_SHAPES = {"ln1w": 16, "ln1b": 16, "ln2w": 16, "ln2b": 16, "scl": 16, "pzr": 16,
                  "pzi": 16, "lcr": 16, "lci": 16, "b1": 64, "b2": 16, "bin": 4,
                  "lnlw": 4, "lnlb": 4, "bout": 32, "maskoh": 8}
    for n, w in PRM_SHAPES.items():
        prm[n] = nc.dram_tensor(f"p_{n}", [128, w], F, kind="ExternalInput").ap()
    eye = nc.dram_tensor("eye_in", [128, 128], BF16, kind="ExternalInput").ap()
    xout = nc.dram_tensor("xout", [4096, NTOK], F, kind="ExternalOutput").ap()

    cbb = nc.dram_tensor("cbb", [65536, 8], BF16).ap()
    wgT_sl = nc.dram_tensor("wgT_sl", [4096, ISLP], BF16).ap()
    wuT_sl = nc.dram_tensor("wuT_sl", [4096, ISLP], BF16).ap()
    wdT_sl = nc.dram_tensor("wdT_sl", [ISLP, 4096], BF16).ap()
    wgT = nc.dram_tensor("wgT", [8 * 4096, ISLP], BF16).ap()
    wuT = nc.dram_tensor("wuT", [8 * 4096, ISLP], BF16).ap()
    wdT = nc.dram_tensor("wdT", [8 * ISLP, 4096], BF16).ap()
    qkvwb = nc.dram_tensor("qkvwb", [4096, 12288], BF16).ap()
    owb = nc.dram_tensor("owb", [4096, 4096], BF16).ap()
    w1qb = nc.dram_tensor("w1qb", [4096, 4096], BF16).ap()
    w1qT = nc.dram_tensor("w1qT", [4096, 4096], BF16).ap()
    winb = nc.dram_tensor("winb", [512, 4096], BF16).ap()
    winT = nc.dram_tensor("winT", [4096, 512], BF16).ap()
    w1b = nc.dram_tensor("w1b", [8192, 512], BF16).ap()
    w1T = nc.dram_tensor("w1T", [2048, 2048], BF16).ap()
    w2b = nc.dram_tensor("w2b", [2048, 2048], BF16).ap()
    w2T = nc.dram_tensor("w2T", [8192, 512], BF16).ap()
    woutb = nc.dram_tensor("woutb", [4096, 512], BF16).ap()
    woutT = nc.dram_tensor("woutT", [512, 4096], BF16).ap()
    z = nc.dram_tensor("z", [8 * ISLP, NTOK], BF16).ap()
    qT = nc.dram_tensor("qT", [4096, NTOK], BF16).ap()
    kT_sl = nc.dram_tensor("kT_sl", [4096, NTOK], BF16).ap()
    kT_full = nc.dram_tensor("kT_full", [32768, NTOK], BF16).ap()
    v_sl = nc.dram_tensor("v_sl", [8192, 512], BF16).ap()
    v_full = nc.dram_tensor("v_full", [65536, 512], BF16).ap()
    agA_in = [nc.dram_tensor(f"agA_in{d}", [1024, 1], F).ap() for d in range(4)]
    agA_out = [nc.dram_tensor(f"agA_out{d}", [8192, 1], F).ap() for d in range(4)]
    xs = [nc.dram_tensor(f"x{i}", [4096, NTOK], F).ap() for i in range(1, 6)]
    x1, x2, x3, x4, x5 = xs

    prep_aps = dict(cb=cb, idxg=idxg, idxu=idxu, idxd=idxd, cbb=cbb,
                    wgT_sl=wgT_sl, wuT_sl=wuT_sl, wdT_sl=wdT_sl,
                    wgT=wgT, wuT=wuT, wdT=wdT)
    cast_jobs = [(qkvw, qkvwb), (ow, owb), (w1q, w1qb), (win, winb),
                 (w1, w1b), (w2, w2b), (wout, woutb)]
    tp_jobs = [(w1qb, w1qT), (winb, winT), (woutb, woutT)]
    for d in range(4):
        tp_jobs.append((w1b[d * 2048:(d + 1) * 2048, :], w1T[d * 512:(d + 1) * 512, :]))
        tp_jobs.append((w2b[d * 512:(d + 1) * 512, :], w2T[d * 2048:(d + 1) * 2048, :]))

    with ExitStack() as stack:
        ctx = Ctx(nc, stack)
        block = stack.enter_context(nc.Block())
        emit_const_init(ctx, block, eye)
        prep_dn = emit_prep(ctx, block, prep_aps, cast_jobs=cast_jobs, tp_jobs=tp_jobs)
        s1 = emit_ffn0(ctx, block, "s1", x0, x1, wgT, wuT, wdT, z, waits=[prep_dn])
        wq = [("p_cj", ctx.prep_totals["cj"]), ("p_wb", ctx.prep_totals["wb"])]
        s2 = emit_attn(ctx, block, "a", x1, x2, qkvwb, qkvb_rs, vb_rs, owb, ob_rs,
                       qT, kT_sl, kT_full, v_sl, v_full, waits=[s1], wwaits=wq[:1])
        s3 = emit_ffn0(ctx, block, "s3", x2, x3, wgT, wuT, wdT, z, waits=[s2])
        s4 = emit_ffn1(ctx, block, "s4", x3, x4, w1qT, scale_rs, bias_rs, waits=[s3],
                       wwaits=wq)
        s5 = emit_l1attn(ctx, block, "s5", x4, x5, winT, w1T, w2T, woutT, prm,
                         agA_in, agA_out, waits=[s4], wwaits=wq)
        s6 = emit_ffn1(ctx, block, "s6", x5, xout, w1qT, scale_rs, bias_rs, waits=[s5],
                       wwaits=wq)
    return nc


def _rs45(a):
    a = np.asarray(a, np.float32)
    return np.ascontiguousarray(a.reshape(4, 4, 128).transpose(2, 0, 1).reshape(128, 16))


_NC_CACHE = {}


def kernel(**inputs):
    from concourse.bass_utils import run_bass_kernel_spmd

    x = np.asarray(inputs["L0_attn_x"])[0]
    cb = np.ascontiguousarray(np.asarray(inputs["L0_ffn_codebook"], np.float32))
    ig = np.asarray(inputs["L0_ffn_idx_gate"])
    iu = np.asarray(inputs["L0_ffn_idx_up"])
    idn = np.asarray(inputs["L0_ffn_idx_down"])
    qkvw = np.ascontiguousarray(np.asarray(inputs["L0_attn_attn_qkvw"], np.float32))
    qkvb = np.asarray(inputs["L0_attn_attn_qkvb"], np.float32)
    ow = np.ascontiguousarray(np.asarray(inputs["L0_attn_attn_ow"], np.float32))
    ob = np.asarray(inputs["L0_attn_attn_ob"], np.float32)
    w1q = np.ascontiguousarray(np.asarray(inputs["L1_ffn_weight_q"]))
    scale = np.asarray(inputs["L1_ffn_scale"], np.float32)
    bias = np.asarray(inputs["L1_ffn_bias"], np.float32)

    qkvb_rs = np.ascontiguousarray(qkvb[:8192].reshape(64, 128).T)
    vb_rs = np.ascontiguousarray(qkvb[8192:].reshape(32, 128).T)
    ob_rs = np.ascontiguousarray(ob.reshape(32, 128).T)
    scale_rs = np.ascontiguousarray(scale.reshape(-1)[:4096].reshape(32, 128).T)
    bias_rs = np.ascontiguousarray(bias.reshape(32, 128).T)
    win = np.ascontiguousarray(np.asarray(inputs["L1_attn_Win"], np.float32))
    w1 = np.ascontiguousarray(np.asarray(inputs["L1_attn_W1"], np.float32).reshape(8192, 512))
    w2 = np.ascontiguousarray(np.asarray(inputs["L1_attn_W2"], np.float32).reshape(2048, 2048))
    wout = np.ascontiguousarray(np.asarray(inputs["L1_attn_Wout"], np.float32))
    eye = np.eye(128, dtype=np.float32).astype(_bf16_np())

    prm_host = {
        "p_ln1w": _rs45(inputs["L1_attn_ln1_w"]), "p_ln1b": _rs45(inputs["L1_attn_ln1_b"]),
        "p_ln2w": _rs45(inputs["L1_attn_ln2_w"]), "p_ln2b": _rs45(inputs["L1_attn_ln2_b"]),
        "p_scl": _rs45(inputs["L1_attn_sc_lin"]),
        "p_pzr": _rs45(inputs["L1_attn_phz_re"]), "p_pzi": _rs45(inputs["L1_attn_phz_im"]),
        "p_lcr": _rs45(inputs["L1_attn_lc_re"]), "p_lci": _rs45(inputs["L1_attn_lc_im"]),
        "p_b1": np.ascontiguousarray(np.asarray(inputs["L1_attn_b1"], np.float32)
                                     .reshape(4, 16, 128).transpose(2, 0, 1).reshape(128, 64)),
        "p_b2": _rs45(inputs["L1_attn_b2"]),
        "p_bin": np.ascontiguousarray(np.asarray(inputs["L1_attn_bin"], np.float32)
                                      .reshape(4, 128).T),
        "p_lnlw": np.ascontiguousarray(np.asarray(inputs["L1_attn_lnl_w"], np.float32)
                                       .reshape(4, 128).T),
        "p_lnlb": np.ascontiguousarray(np.asarray(inputs["L1_attn_lnl_b"], np.float32)
                                       .reshape(4, 128).T),
        "p_bout": np.ascontiguousarray(np.asarray(inputs["L1_attn_bout"], np.float32)
                                       .reshape(32, 128).T),
    }

    if "nc" not in _NC_CACHE:
        _NC_CACHE["nc"] = build_full()
    nc = _NC_CACHE["nc"]

    in_maps = []
    for r in range(8):
        g = np.pad(ig[r * ISL:(r + 1) * ISL].astype(np.uint32), ((0, ISLP - ISL), (0, 0)))
        u = np.pad(iu[r * ISL:(r + 1) * ISL].astype(np.uint32), ((0, ISLP - ISL), (0, 0)))
        dslice = np.ascontiguousarray(idn[:, r * DSL:(r + 1) * DSL].astype(np.uint32))
        moh = np.zeros((128, 8), np.float32)
        moh[:, r] = 1.0
        m = {"x0": np.ascontiguousarray(x[r * NTOK:(r + 1) * NTOK, :].T),
             "cb": cb, "idxg": g, "idxu": u, "idxd": dslice,
             "qkvw": qkvw, "ow": ow, "qkvb_rs": qkvb_rs, "vb_rs": vb_rs, "ob_rs": ob_rs,
             "w1q": w1q, "scale_rs": scale_rs, "bias_rs": bias_rs,
             "win": win, "w1": w1, "w2": w2, "wout": wout,
             "eye_in": eye.view(np.uint16), "p_maskoh": moh}
        m.update(prm_host)
        in_maps.append(m)
    res = run_bass_kernel_spmd(nc, in_maps, core_ids=list(range(8)))
    out = np.empty((1, 8192, 4096), np.float32)
    for r in range(8):
        out[0, r * NTOK:(r + 1) * NTOK, :] = res.results[r]["xout"].T
    return out


def _bf16_np():
    import ml_dtypes
    return ml_dtypes.bfloat16
